# version 16 (final)
# baseline (speedup 1.0000x reference)
import contextlib

import numpy as np
import ml_dtypes
import concourse.bass as bass
import concourse.mybir as mybir
from concourse.bass_utils import run_bass_kernel_spmd

F32 = mybir.dt.float32
BF16 = mybir.dt.bfloat16
AF = mybir.ActivationFunctionType
ALU = mybir.AluOpType
AX = mybir.AxisListType
BF = ml_dtypes.bfloat16

B, T, C = 2, 8192, 1024
NCORE = 8
TOK = 2048
NCC = 8
NRING = 8
RK = 1024
ATT_SCALE = 1.0 / 32.0
ESH = 8.0
EPS = 1e-5

SEMS = ("mm", "sc", "ve",
        "sxc0", "sxc1", "sxc2", "sxc3",
        "swv",
        "sxo",
        "sxt",
        "scn",
        "st00", "st01", "st02", "st03",
        "st10", "st11", "st12", "st13",
        "skv", "skv2", "skv3", "skv4",
        "swh",
        "swh2", "swh3",
        "cc",
        "klg", "klg2", "klg3", "klg4",
        "ssp", "ssp2",
        "son", "son2", "son3", "son4",
        "sa0", "sa1",
        "sw10", "sw11", "sw20", "sw21",
        "sw10b", "sw11b", "sw20b", "sw21b",
        "so0", "so1", "so2", "so3",
        "so4", "so5", "so6", "so7")


def _blocks_for_core(i):
    return [8 * m + (i if m % 2 == 0 else 7 - i) for m in range(8)]


def _has_neg(key):
    if isinstance(key, tuple):
        return any(isinstance(k, int) and k < 0 for k in key)
    return False


class Emitter:

    def __init__(self, nc, sems):
        self.nc = nc
        self.sems = sems

    def begin(self, lab, eng):
        self.lab = lab
        self.eng = eng
        self.cnt = {s: 0 for s in SEMS}
        self.ck = {}
        self.lastwait = {}

    def wait(self, engine, sem, val):
        if val is None or val <= 0:
            return
        if self.lab == engine:
            if self.lastwait.get(sem, -1) >= val:
                return
            self.lastwait[sem] = val
            self.eng.wait_ge(self.sems[sem], val)

    def waitck(self, engine, key, sem):
        if key not in self.ck:
            if _has_neg(key):
                return
            raise KeyError(f"waitck: missing checkpoint {key!r}")
        self.wait(engine, sem, self.ck[key][sem])

    def waitall(self, engine, sem):
        self.wait(engine, sem, self.cnt[sem])

    def op(self, engine, build, inc=None, incn=1):
        if engine == "vector":
            self.wait(engine, "ve", self.cnt["ve"])
        elif engine == "scalar" and inc == "sc":
            self.wait(engine, "sc", self.cnt["sc"])
        if self.lab == engine:
            ins = build(self.eng)
            if inc is not None:
                ins.then_inc(self.sems[inc], incn)
        if inc is not None:
            self.cnt[inc] += incn

    def dma(self, engine, out, in_, inc, transpose=False, slow=False):
        self.wait(engine, inc, self.cnt[inc])
        if slow:
            self.op(engine,
                    lambda e: e.dma_start(out, in_,
                                          allow_slow_non_contiguous=True),
                    inc=inc, incn=16)
        else:
            self.op(engine, lambda e: e.dma_start(out, in_, transpose=transpose),
                    inc=inc, incn=16)

    def mark(self, key):
        self.ck[key] = dict(self.cnt)

    def at(self, key, sem):
        return self.ck[key][sem] if key in self.ck else 0


def build():
    nc = bass.Bass("TRN2", target_bir_lowering=False)

    xT = nc.dram_tensor("xT", [C, T], F32, kind="ExternalInput")
    xown = nc.dram_tensor("xown", [TOK, C], F32, kind="ExternalInput")
    xownT = nc.dram_tensor("xownT", [C, TOK], F32, kind="ExternalInput")
    wattn = nc.dram_tensor("wattn", [C, 3 * C], F32, kind="ExternalInput")
    battn = nc.dram_tensor("battn", [3 * C], F32, kind="ExternalInput")
    wproj = nc.dram_tensor("wproj", [C, C], F32, kind="ExternalInput")
    bproj = nc.dram_tensor("bproj", [C], F32, kind="ExternalInput")
    wg = nc.dram_tensor("wg", [C, 8], F32, kind="ExternalInput")
    bg = nc.dram_tensor("bg", [8], F32, kind="ExternalInput")
    w1 = nc.dram_tensor("w1", [8, C, C], F32, kind="ExternalInput")
    b1 = nc.dram_tensor("b1", [8, C], F32, kind="ExternalInput")
    w2 = nc.dram_tensor("w2", [8, C, C], F32, kind="ExternalInput")
    b2 = nc.dram_tensor("b2", [8, C], F32, kind="ExternalInput")
    qmask = nc.dram_tensor("qmask", [64 * 128, 128], BF16, kind="ExternalInput")

    out = nc.dram_tensor("out", [TOK, C], F32, kind="ExternalOutput")

    wbp = nc.dram_tensor("wbp", [C, C], BF16)
    wb1 = nc.dram_tensor("wb1", [8, C, C], BF16)
    wb2 = nc.dram_tensor("wb2", [8, C, C], BF16)
    wbg = nc.dram_tensor("wbg", [C, 8], BF16)
    KVM = 128 * NCC * RK
    kvsend = nc.dram_tensor("kvsend", [NRING, 2, 128, NCC * RK], BF16)
    kvgath = nc.dram_tensor("kvgath", [NRING, 2, 2, 128, NCC * RK], BF16)

    consts_np = np.zeros((128, 2), np.float32)
    consts_np[:, 0] = -ESH
    consts_np[:, 1] = EPS
    consts_c = nc.inline_tensor(consts_np, "fconsts_c")
    ones_r = nc.inline_tensor(np.ones((1, 128), BF), "onesr_c")
    ones_sq = nc.inline_tensor(np.ones((128, 128), BF), "ones128_c")
    ident_c = nc.inline_tensor(np.eye(128).astype(BF), "ident_c")
    ehot_np = np.zeros((8, 8, 128), np.float32)
    for e_ in range(8):
        ehot_np[e_, e_, :] = 1.0
    ehot_c = nc.inline_tensor(ehot_np.transpose(1, 0, 2).astype(BF), "ehot_c")

    ctx = contextlib.ExitStack()
    sems = {s: ctx.enter_context(nc.semaphore(s)) for s in SEMS}
    em = Emitter(nc, sems)

    chunks = [(b, a) for b in range(B) for a in range(NRING)]

    persist_ck = {}
    persist_cnt = {s: 0 for s in SEMS}

    def run_block(schedule_fn):
        saved = {}
        with nc.Block(no_gpsimd_drain=True) as blk:
            for lab, reg in (("tensor", blk.tensor), ("scalar", blk.scalar),
                             ("vector", blk.vector), ("sync", blk.sync),
                             ("gpsimd", blk.gpsimd)):
                def f(eng, lab=lab):
                    em.begin(lab, eng)
                    em.ck.update(persist_ck)
                    em.cnt.update(persist_cnt)
                    schedule_fn()
                    saved["ck"] = dict(em.ck)
                    saved["cnt"] = dict(em.cnt)
                reg(f)
        persist_ck.update(saved["ck"])
        persist_cnt.update(saved["cnt"])

    with ctx:
        with nc.Block(no_gpsimd_drain=True) as _blk0:
            def _clr(eng):
                for _s in SEMS:
                    eng.sem_clear(sems[_s])
            _blk0.sync(_clr)
        onesr = ctx.enter_context(nc.sbuf_tensor("onesr", [1, 128], BF16))
        cst_sb = ctx.enter_context(nc.sbuf_tensor("cst_sb", [128, 2], F32))
        nc.const_aps.register(F32, -ESH, cst_sb[:, 0:1])
        nc.const_aps.register(F32, EPS, cst_sb[:, 1:2])
        if True:
            scopeA = contextlib.ExitStack()
            with scopeA:
                Qp = scopeA.enter_context(nc.sbuf_tensor("Qp", [128, NCC, TOK], BF16))
                wk_sb = scopeA.enter_context(nc.sbuf_tensor("wk_sb", [128, NCC, C], BF16))
                wv_sb = scopeA.enter_context(nc.sbuf_tensor("wv_sb", [128, NCC, C], BF16))
                masks = scopeA.enter_context(nc.sbuf_tensor("masks", [128, 64, 128], BF16))
                ones128 = scopeA.enter_context(nc.sbuf_tensor("ones128", [128, 128], BF16))
                bq_sb = scopeA.enter_context(nc.sbuf_tensor("bq_sb", [128, NCC], F32))
                bk_sb = scopeA.enter_context(nc.sbuf_tensor("bk_sb", [128, NCC], F32))
                bvrow = scopeA.enter_context(nc.sbuf_tensor("bvrow", [1, C], F32))
                bvrow_b = scopeA.enter_context(nc.sbuf_tensor("bvrow_b", [1, C], BF16))
                bvb = scopeA.enter_context(nc.sbuf_tensor("bvb", [128, C], F32))

                scope0 = contextlib.ExitStack()
                with scope0:
                    wq_sb = scope0.enter_context(nc.sbuf_tensor("wq_sb", [128, NCC, C], BF16))
                    xto = scope0.enter_context(nc.sbuf_tensor("xto", [128, NCC, TOK], BF16))
                    qps0 = scope0.enter_context(nc.psum_tensor("qps0", [128, 512], F32))
                    qps1 = scope0.enter_context(nc.psum_tensor("qps1", [128, 512], F32))
                    bvps = scope0.enter_context(nc.psum_tensor("bvps", [128, C], F32))
                    qps = [qps0, qps1]

                    def sched0():
                        for q in range(4):
                            em.dma("gpsimd", xto[:, :, q * 512:(q + 1) * 512],
                                   bass.AP(xownT, q * 512,
                                           [[TOK, 128], [128 * TOK, NCC],
                                            [1, 512]]),
                                   "st0%d" % q)
                            em.mark(("xtp", q))
                        for name, sb, co0 in (("wq", wq_sb, 0), ("wk", wk_sb, C),
                                              ("wv", wv_sb, 2 * C)):
                            em.dma("gpsimd", sb[:, :, :],
                                   bass.AP(wattn, co0,
                                           [[3 * C, 128], [128 * 3 * C, NCC], [1, C]]),
                                   "swv")
                        em.mark("wqkv")

                        em.dma("sync", cst_sb[:, :], consts_c.ap(), "scn")
                        em.dma("sync", onesr[:, :], ones_r.ap(), "scn")
                        em.dma("sync", ones128[:, :], ones_sq.ap(), "scn")
                        em.dma("sync", masks[:, :, :],
                               bass.AP(qmask, 0, [[128, 128], [128 * 128, 64], [1, 128]]),
                               "scn")
                        em.dma("sync", bq_sb[:, :],
                               bass.AP(battn, 0, [[1, 128], [128, NCC]]), "scn",
                               slow=True)
                        em.dma("sync", bk_sb[:, :],
                               bass.AP(battn, C, [[1, 128], [128, NCC]]), "scn",
                               slow=True)
                        em.dma("sync", bvrow[:, :],
                               bass.AP(battn, 2 * C, [[1, 1], [1, C]]), "scn")
                        em.mark("consts")

                        em.waitck("scalar", "consts", "scn")
                        em.op("scalar", lambda e: e.copy(bvrow_b[:, :], bvrow[:, :]),
                              inc="sc")
                        em.mark("bvrow_b")

                        em.waitck("tensor", "wqkv", "swv")
                        em.waitck("tensor", "consts", "scn")
                        for co in range(NCC):
                            for qs in range(TOK // 512):
                                idx = co * 4 + qs
                                r = idx % 2
                                em.waitck("tensor", ("xtp", qs), "st0%d" % qs)
                                em.waitck("tensor", ("qev", idx - 2), "sc")
                                for ci in range(NCC):
                                    em.op("tensor", lambda e, co=co, qs=qs, ci=ci, r=r:
                                          e.matmul(qps[r][:, :],
                                                   wq_sb[:, ci, co * 128:(co + 1) * 128],
                                                   xto[:, ci, qs * 512:(qs + 1) * 512],
                                                   start=(ci == 0), stop=(ci == 7)),
                                          inc=("mm" if ci == 7 else None))
                                em.mark(("qmm", idx))
                                em.waitck("scalar", ("qmm", idx), "mm")
                                em.op("scalar", lambda e, co=co, qs=qs, r=r:
                                      e.activation(Qp[:, co, qs * 512:(qs + 1) * 512],
                                                   qps[r][:, :], AF.Identity,
                                                   bias=bq_sb[:, co:co + 1]),
                                      inc="sc")
                                em.mark(("qev", idx))
                        em.waitck("tensor", "bvrow_b", "sc")
                        for h in range(2):
                            em.op("tensor", lambda e, h=h:
                                  e.matmul(bvps[:, h * 512:(h + 1) * 512], onesr[:, :],
                                           bvrow_b[:, h * 512:(h + 1) * 512],
                                           start=True, stop=True),
                                  inc=("mm" if h == 1 else None))
                        em.mark("bvmm")
                        em.waitck("scalar", "bvmm", "mm")
                        em.op("scalar", lambda e: e.activation(bvb[:, :], bvps[:, :],
                                                               AF.Copy), inc="sc")
                        em.mark("bvb")

                    run_block(sched0)

                scope1 = contextlib.ExitStack()
                with scope1:
                    xt0 = scope1.enter_context(nc.sbuf_tensor("xt0", [128, NCC, RK], BF16))
                    xt1 = scope1.enter_context(nc.sbuf_tensor("xt1", [128, NCC, RK], BF16))
                    Ksb = scope1.enter_context(nc.sbuf_tensor("Ksb", [128, NCC, RK], BF16))
                    Vsb = scope1.enter_context(nc.sbuf_tensor("Vsb", [128, NCC, RK], BF16))
                    Psb = scope1.enter_context(nc.sbuf_tensor("Psb", [128, NCC, RK], BF16))
                    Op = scope1.enter_context(nc.sbuf_tensor("Op", [128, NCC, 1024], F32))
                    lsb = scope1.enter_context(nc.sbuf_tensor("lsb", [128, 1024], F32))
                    pK0 = scope1.enter_context(nc.psum_tensor("pK0", [128, RK], F32))
                    pK1 = scope1.enter_context(nc.psum_tensor("pK1", [128, RK], F32))
                    pV0 = scope1.enter_context(nc.psum_tensor("pV0", [128, RK], F32))
                    pV1 = scope1.enter_context(nc.psum_tensor("pV1", [128, RK], F32))
                    xts = [xt0, xt1]
                    pK = [pK0, pK1]
                    pV = [pV0, pV1]
                    rkv = scope1.enter_context(nc.sync.register("rkv"))
                    rtmp = scope1.enter_context(nc.sync.register("rtmp"))

                    AT = nc.alloc_sbuf_tensor_at
                    adr = lambda h: nc.lookup_mloc(h).addr
                    on0m = AT("on0m", [128, NCC, 1024], BF16, offset=adr(wv_sb))
                    on1m = AT("on1m", [128, NCC, 1024], BF16, offset=adr(masks))
                    wpm = AT("wpm", [128, NCC, C], BF16, offset=adr(wk_sb))
                    xst0 = AT("xst0", [128, C], F32, offset=adr(bq_sb))
                    xst1 = AT("xst1", [128, C], F32, offset=adr(bvrow_b))
                    bpjrow_b = AT("bpjrow_b", [1, C], BF16,
                                  offset=adr(bvrow_b) + 4096)
                    assert adr(xt1) == adr(xt0) + 16384
                    assert adr(Ksb) == adr(xt0) + 2 * 16384
                    assert adr(Vsb) == adr(xt0) + 3 * 16384
                    yacc_m = AT("yacc", [128, 16, C], F32, offset=adr(xt0))
                    sqb0 = AT("sqb0", [128, C], F32, offset=adr(Psb))
                    lnt = AT("lnt", [128, 16, 8], F32, offset=adr(Psb) + 4096)
                    identb2 = AT("identb2", [128, 128], BF16,
                                 offset=adr(Psb) + 4608)
                    x1c0 = AT("x1c0", [128, C], BF16, offset=adr(Psb) + 4864)
                    x1c1 = AT("x1c1", [128, C], BF16, offset=adr(Psb) + 6912)
                    w1a = AT("w1a", [128, NCC, C], BF16, offset=adr(wv_sb))
                    w1b = AT("w1b", [128, NCC, C], BF16, offset=adr(wk_sb))
                    w2a = AT("w2a", [128, NCC, C], BF16, offset=adr(masks))
                    w2b = AT("w2b", [128, NCC, C], BF16,
                             offset=adr(Psb) + 8960)
                    hsb = AT("hsb", [128, NCC, 1024], BF16,
                             offset=adr(Psb) + 8960 + 16384)
                    sqc0 = AT("sqc0", [128, C], F32,
                              offset=adr(Psb) + 8960 + 2 * 16384)
                    sqc1 = AT("sqc1", [128, C], F32,
                              offset=adr(Psb) + 8960 + 2 * 16384 + 4096)
                    lnt2 = AT("lnt2", [128, 16, 8], F32,
                              offset=adr(Psb) + 8960 + 2 * 16384 + 2 * 4096)
                    _sm = adr(Psb) + 8960 + 2 * 16384 + 2 * 4096 + 512
                    wg_sb = AT("wg_sb", [128, NCC, 8], BF16, offset=_sm)
                    bgrow_b = AT("bgrow_b", [1, 8], BF16, offset=_sm + 128)
                    b1_sb = AT("b1_sb", [128, 8, 8], F32, offset=_sm + 160)
                    identb = AT("identb", [128, 128], BF16, offset=_sm + 416)
                    lg = AT("lg", [128, 16, 8], F32, offset=_sm + 672)
                    mstore = AT("mstore", [128, 16, 8], F32, offset=_sm + 1184)
                    mxc = AT("mxc", [128, 1], F32, offset=_sm + 1696)
                    mtb = AT("mtb", [128, 8], BF16, offset=_sm + 1728)
                    maskT = AT("maskT", [8, 16, 128], BF16,
                               offset=adr(Psb) + 4864)
                    b2r_sb = AT("b2r_sb", [8, C], BF16, offset=adr(Psb))
                    assert _sm + 1760 <= adr(lsb) + 4096
                    nc._b2_handles = (on0m, on1m, wpm, xst0, xst1, bpjrow_b,
                                      yacc_m, sqb0, lnt, identb2, x1c0, x1c1,
                                      w1a, w1b, w2a, w2b, hsb, sqc0, sqc1,
                                      lnt2, wg_sb, bgrow_b, b1_sb, identb, lg,
                                      mstore, mxc, mtb, maskT, b2r_sb)

                    def sched1():
                        i_batchA = chunks[0][0]

                        def xt_prefetch(n):
                            b, a = chunks[n]
                            if n >= 2:
                                em.waitck("gpsimd", ("kvmm", n - 2), "mm")
                            for p in range(2):
                                em.dma("gpsimd",
                                       xts[n % 2][:, p * 4:(p + 1) * 4, :],
                                       bass.AP(xT, a * RK + p * 4 * 128 * T,
                                               [[T, 128], [128 * T, 4],
                                                [1, RK]]),
                                       "st%d%d" % (n % 2, p))
                            em.mark(("xt", n))

                        def emit_cc(a):
                            for _s in ("skv", "skv2", "skv3", "skv4"):
                                em.waitck("gpsimd", ("kvsp", a), _s)
                            em.op("gpsimd", lambda e, a=a:
                                  e.collective_compute(
                                      "AllGather", ALU.bypass,
                                      replica_groups=[[0, 1], [2, 3], [4, 5], [6, 7]],
                                      ins=[kvsend[a, :, :, :].opt()],
                                      outs=[kvgath[a, :, :, :, :].opt()]),
                                  inc="cc")
                            em.mark(("ccd", a))

                        def kld_load(ring, dstK, dstV):
                            npc = 2 if ring < 2 else 1
                            w = NCC // npc
                            for kv, dstb in ((0, dstK), (1, dstV)):
                                for p in range(npc):
                                    if npc == 2:
                                        s = (("klg", "klg2") if kv == 0
                                             else ("klg3", "klg4"))[p]
                                    else:
                                        s = (("klg", "klg2") if kv == 0
                                             else ("klg3", "klg4"))[ring % 2]
                                    em.wait("sync", s, em.cnt[s])
                                    em.op("sync",
                                          lambda e, ring=ring, kv=kv, p=p,
                                          dstb=dstb, w=w:
                                          (e.reg_alu(rkv, rtmp,
                                                     (ring * 4 + kv) * KVM
                                                     + p * w * RK, ALU.add),
                                           e.dma_start(
                                               dstb[:, p * w:(p + 1) * w, :],
                                               bass.AP(kvgath, rkv,
                                                       [[NCC * RK, 128],
                                                        [1, w * RK]]))
                                           )[1],
                                          inc=s, incn=16)

                        xt_prefetch(0)
                        xt_prefetch(1)

                        em.waitck("tensor", "consts", "scn")
                        em.waitck("vector", "consts", "scn")
                        em.waitck("scalar", "consts", "scn")

                        def g_slotoff(e):
                            pid = nc.partition_id(engines=[mybir.EngineType.SP])
                            e.reg_alu(rtmp, pid, 1, ALU.bitwise_and)
                            e.reg_alu(rtmp, rtmp, 1, ALU.bitwise_xor)
                            return e.reg_alu(rtmp, rtmp, 2 * KVM, ALU.mult)
                        em.op("sync", g_slotoff, inc=None)

                        for n, (b, a) in enumerate(chunks):
                            p2 = n % 2
                            qoff = a * 128
                            Nq = 1024 - qoff
                            nqs = (Nq + 511) // 512
                            batchA = n < NRING

                            if batchA:
                                for _l in range(2):
                                    em.waitck("tensor", ("xt", n),
                                              "st%d%d" % (p2, _l))
                                em.waitck("tensor", ("exps", n - 1), "sc")
                                for cc in range(NCC):
                                    em.waitck("tensor", ("kev", n, cc - 2), "sc")
                                    for h in range(2):
                                        for ci in range(NCC):
                                            em.op("tensor", lambda e, cc=cc, h=h, ci=ci, p2=p2:
                                                  e.matmul(pK[cc % 2][:, h * 512:(h + 1) * 512],
                                                           wk_sb[:, ci, cc * 128:(cc + 1) * 128],
                                                           xts[p2][:, ci, h * 512:(h + 1) * 512],
                                                           start=(ci == 0), stop=(ci == 7)),
                                                  inc=("mm" if (h == 1 and ci == 7) else None))
                                    em.mark(("kmm", n, cc))
                                    em.waitck("scalar", ("kmm", n, cc), "mm")
                                    if cc == 0:
                                        for _s in ("skv", "skv2", "skv3",
                                                    "skv4"):
                                            em.waitck("scalar",
                                                      ("kvsp", n - 1), _s)
                                    em.op("scalar", lambda e, cc=cc:
                                          e.activation(Ksb[:, cc, :], pK[cc % 2][:, :],
                                                       AF.Identity, bias=bk_sb[:, cc:cc + 1]),
                                          inc="sc")
                                    em.mark(("kev", n, cc))
                                    if cc in (3, 7):
                                        em.waitck("sync", ("kev", n, cc), "sc")
                                        em.dma("sync",
                                               kvsend[a, 0, :, (cc - 3) * 1024:
                                                      (cc + 1) * 1024],
                                               Ksb[:, cc - 3:cc + 1, :],
                                               "skv" if cc == 3 else "skv3")

                                em.waitck("tensor", ("avacc", n - 1), "ve")
                                em.waitck("vector", "bvb", "sc")
                                for kb in range(NCC):
                                    em.waitck("tensor", ("vev", n, kb - 2), "ve")
                                    for h in range(2):
                                        for ci in range(NCC):
                                            em.op("tensor", lambda e, kb=kb, h=h, ci=ci, p2=p2:
                                                  e.matmul(pV[kb % 2][:, h * 512:(h + 1) * 512],
                                                           xts[p2][:, ci, kb * 128:(kb + 1) * 128],
                                                           wv_sb[:, ci, h * 512:(h + 1) * 512],
                                                           start=(ci == 0), stop=(ci == 7)),
                                                  inc=("mm" if (h == 1 and ci == 7) else None))
                                    em.mark(("vmm", n, kb))
                                    em.waitck("vector", ("vmm", n, kb), "mm")
                                    if kb == 0:
                                        for _s in ("skv", "skv2", "skv3",
                                                    "skv4"):
                                            em.waitck("vector",
                                                      ("kvsp", n - 1), _s)
                                    em.op("vector", lambda e, kb=kb:
                                          e.tensor_tensor(Vsb[:, kb, :], pV[kb % 2][:, :],
                                                          bvb[:, :], ALU.add),
                                          inc="ve")
                                    em.mark(("vev", n, kb))
                                    if kb in (3, 7):
                                        em.waitck("sync", ("vev", n, kb), "ve")
                                        em.dma("sync",
                                               kvsend[a, 1, :, (kb - 3) * 1024:
                                                      (kb + 1) * 1024],
                                               Vsb[:, kb - 3:kb + 1, :],
                                               "skv2" if kb == 3 else "skv4")
                                em.mark(("kvmm", n))
                                em.mark(("kvsp", n))

                                if n + 2 < NRING:
                                    xt_prefetch(n + 2)
                                if a < 6:
                                    emit_cc(a)
                                kvK, kvV = Ksb, Vsb
                            else:
                                kvK = xt0 if a % 2 == 0 else Ksb
                                kvV = xt1 if a % 2 == 0 else Vsb
                                if a == 0:
                                    for ap_ in (0, 1):
                                        if ap_ == 0:
                                            em.waitck("gpsimd",
                                                      ("kvmm", NRING - 1), "mm")
                                        else:
                                            em.waitck("sync",
                                                      ("avmm", NRING - 1, 8), "mm")
                                            for _s in ("skv", "skv2", "skv3",
                                                        "skv4"):
                                                em.waitck("sync",
                                                          ("kvsp", NRING - 1),
                                                          _s)
                                        em.waitck("sync", ("ccd", ap_), "cc")
                                        bK = xt0 if ap_ % 2 == 0 else Ksb
                                        bV = xt1 if ap_ % 2 == 0 else Vsb
                                        kld_load(ap_, bK, bV)
                                        em.mark(("kld", ap_))
                                if a == 0:
                                    emit_cc(6)
                                    emit_cc(7)
                                for _s in ("klg", "klg2", "klg3", "klg4"):
                                    em.waitck("tensor", ("kld", a), _s)
                                em.mark(("kvmm", n))

                            for kb in range(NCC):
                                if batchA:
                                    if kb < 2:
                                        em.waitck("tensor", ("kev", n, 7), "sc")
                                    else:
                                        em.waitck("tensor", ("sexp", n, kb - 2), "sc")
                                else:
                                    if kb >= 2:
                                        em.waitck("tensor", ("sexp", n, kb - 2), "sc")
                                for qs in range(nqs):
                                    w = min(512, Nq - qs * 512)
                                    for cc in range(NCC):
                                        em.op("tensor", lambda e, kb=kb, qs=qs, w=w, cc=cc, b=b, qoff=qoff, kvK=kvK:
                                              e.matmul(pK[kb % 2][:, qs * 512:qs * 512 + w],
                                                       kvK[:, cc, kb * 128:(kb + 1) * 128],
                                                       Qp[:, cc, b * 1024 + qoff + qs * 512:
                                                          b * 1024 + qoff + qs * 512 + w],
                                                       start=(cc == 0), stop=(cc == 7)),
                                              inc=("mm" if (cc == 7 and qs == nqs - 1) else None))
                                em.mark(("smm", n, kb))
                                em.waitck("scalar", ("smm", n, kb), "mm")
                                em.op("scalar", lambda e, kb=kb, Nq=Nq:
                                      e.activation(Psb[:, kb, 0:Nq], pK[kb % 2][:, 0:Nq],
                                                   AF.Exp, bias=-ESH, scale=ATT_SCALE),
                                      inc="sc")
                                em.mark(("sexp", n, kb))
                                em.waitck("vector", ("sexp", n, kb), "sc")
                                em.op("vector", lambda e, kb=kb, a=a:
                                      e.tensor_mul(Psb[:, kb, 0:128], Psb[:, kb, 0:128],
                                                   masks[:, a * 8 + kb, :]),
                                      inc="ve")
                                em.mark(("pmask", n, kb))
                            em.mark(("exps", n))

                            for r in range(9):
                                if r < 2:
                                    em.waitck("tensor", ("pmask", n, 7), "ve")
                                else:
                                    em.waitck("tensor", ("avac", n, r - 2), "ve")
                                for qs in range(nqs):
                                    w = min(512, Nq - qs * 512)
                                    for kb in range(NCC):
                                        def mk(e, r=r, qs=qs, w=w, kb=kb, kvV=kvV):
                                            lhsT = (kvV[:, kb, r * 128:(r + 1) * 128]
                                                    if r < 8 else ones128[:, :])
                                            return e.matmul(
                                                pV[r % 2][:, qs * 512:qs * 512 + w],
                                                lhsT,
                                                Psb[:, kb, qs * 512:qs * 512 + w],
                                                start=(kb == 0), stop=(kb == 7))
                                        em.op("tensor", mk,
                                              inc=("mm" if (kb == 7 and qs == nqs - 1) else None))
                                em.mark(("avmm", n, r))
                                em.waitck("vector", ("avmm", n, r), "mm")
                                dst = (Op[:, r, qoff:1024] if r < 8
                                       else lsb[:, qoff:1024])
                                if a == 0:
                                    em.op("vector", lambda e, dst=dst, r=r, Nq=Nq:
                                          e.tensor_copy(dst, pV[r % 2][:, 0:Nq]),
                                          inc="ve")
                                else:
                                    em.op("vector", lambda e, dst=dst, r=r, Nq=Nq:
                                          e.tensor_add(dst, dst, pV[r % 2][:, 0:Nq]),
                                          inc="ve")
                                em.mark(("avac", n, r))
                            em.mark(("avacc", n))

                            if not batchA and a + 2 < NRING:
                                em.waitck("sync", ("ccd", a + 2), "cc")
                                em.waitck("sync", ("avmm", n, 8), "mm")
                                bK = xt0 if a % 2 == 0 else Ksb
                                bV = xt1 if a % 2 == 0 else Vsb
                                kld_load(a + 2, bK, bV)
                                em.mark(("kld", a + 2))

                            if a == NRING - 1:
                                em.wait("vector", "ve", em.cnt["ve"])
                                em.op("vector", lambda e: e.reciprocal(lsb[:, :],
                                                                       lsb[:, :]),
                                      inc="ve")
                                dst_on = on0m if b == i_batchA else on1m
                                for cc in range(NCC):
                                    em.op("vector", lambda e, cc=cc, dst_on=dst_on:
                                          e.tensor_mul(dst_on[:, cc, :],
                                                       Op[:, cc, :], lsb[:, :]),
                                          inc="ve")
                                em.mark(("onrm", b))

                    run_block(sched1)

            x1t = ctx.enter_context(nc.sbuf_tensor("x1t", [128, NCC, TOK], BF16))
            yacc = yacc_m

            scopeB = contextlib.ExitStack()
            with scopeB:
                _a = scopeB.enter_context
                pj0 = _a(nc.psum_tensor("pj0", [128, C], F32))
                pj1 = _a(nc.psum_tensor("pj1", [128, C], F32))
                pj2 = _a(nc.psum_tensor("pj2", [128, C], F32))
                ptx0 = _a(nc.psum_tensor("ptx0", [128, 128], BF16))
                ptx1 = _a(nc.psum_tensor("ptx1", [128, 128], BF16))
                xst = [xst0, xst1]
                x1c = [x1c0, x1c1]
                pj = [pj0, pj1, pj2]
                ptx = [ptx0, ptx1]

                def sched2():
                    em.waitck("gpsimd", ("kvmm", NRING - 1), "mm")
                    em.waitck("gpsimd", ("kvsp", NRING - 1), "sc")
                    em.waitck("gpsimd", ("kvsp", NRING - 1), "ve")
                    em.dma("gpsimd", bpjrow_b[:, :],
                           bass.AP(bproj, 0, [[1, 1], [1, C]]), "swh")
                    em.mark("bpjb")
                    em.dma("gpsimd", wpm[:, 0:4, :],
                           bass.AP(wproj, 0, [[C, 128], [128 * C, 4], [1, C]]),
                           "swh2")
                    em.dma("gpsimd", wpm[:, 4:8, :],
                           bass.AP(wproj, 4 * 128 * C,
                                   [[C, 128], [128 * C, 4], [1, C]]),
                           "swh3")
                    em.mark("wproj")
                    for t0 in range(2):
                        em.dma("gpsimd", xst[t0][:, :],
                               xown[t0 * 128:(t0 + 1) * 128, :], "sa%d" % t0)
                        em.mark(("xst", t0))
                    em.waitck("scalar", ("avmm", 15, 8), "mm")
                    em.dma("scalar", identb2[:, :], ident_c.ap(), "scn")
                    em.mark("bpjld")

                    def emit_tr(t):
                        em.waitck("tensor", ("x1c", t), "ve")
                        em.waitck("tensor", "bpjld", "scn")
                        for cc in range(NCC):
                            if cc >= 2:
                                em.waitck("tensor", ("x1tev", t, cc - 2), "sc")
                            else:
                                em.waitck("tensor", ("x1tev", t - 1, 6 + cc), "sc")
                            em.op("tensor", lambda e, cc=cc, t=t:
                                  e.transpose(ptx[cc % 2][:, :],
                                              x1c[t % 2][:, cc * 128:(cc + 1) * 128],
                                              identb2[:, :]),
                                  inc="mm")
                            em.mark(("x1tmm", t, cc))
                            em.waitck("scalar", ("x1tmm", t, cc), "mm")
                            em.op("scalar", lambda e, t=t, cc=cc:
                                  e.activation(x1t[:, cc, t * 128:(t + 1) * 128],
                                               ptx[cc % 2][:, :], AF.Copy),
                                  inc="sc")
                            em.mark(("x1tev", t, cc))

                    for t in range(16):
                        par = t % 2
                        pq = t % 3
                        bb, tt = t // 8, t % 8
                        em.waitck("tensor", ("onrm", bb), "ve")
                        em.waitck("tensor", "wproj", "swh2")
                        em.waitck("tensor", "wproj", "swh3")
                        em.waitck("tensor", ("pjfree", t - 3), "ve")
                        em.waitck("tensor", "bpjb", "swh")
                        for h in range(2):
                            for cc in range(NCC):
                                onsrc = on0m if bb == 0 else on1m
                                em.op("tensor", lambda e, h=h, cc=cc, tt=tt, pq=pq, onsrc=onsrc:
                                      e.matmul(pj[pq][:, h * 512:(h + 1) * 512],
                                               onsrc[:, cc, tt * 128:(tt + 1) * 128],
                                               wpm[:, cc, h * 512:(h + 1) * 512],
                                               start=(cc == 0), stop=False),
                                      inc=None)
                            em.op("tensor", lambda e, h=h, pq=pq:
                                  e.matmul(pj[pq][:, h * 512:(h + 1) * 512],
                                           onesr[:, :],
                                           bpjrow_b[:, h * 512:(h + 1) * 512],
                                           start=False, stop=True),
                                  inc=("mm" if h == 1 else None))
                        em.mark(("pjmm", t))

                        if t >= 2:
                            emit_tr(t - 2)

                        em.waitck("vector", ("pjmm", t), "mm")
                        em.waitck("vector", ("xst", t), "sa%d" % par)
                        em.op("vector", lambda e, t=t, par=par, pq=pq:
                              e.scalar_tensor_tensor(yacc[:, t, :], pj[pq][:, :],
                                                     1.0, xst[par][:, :],
                                                     ALU.mult, ALU.add,
                                                     accum_out=lnt[:, t, 0:1]),
                              inc="ve")
                        em.mark(("rtdone", t))
                        em.mark(("pjfree", t))
                        em.op("vector", lambda e, t=t:
                              e.scalar_tensor_tensor(sqb0[:, :], yacc[:, t, :],
                                                     1.0, yacc[:, t, :],
                                                     ALU.mult, ALU.mult,
                                                     accum_out=lnt[:, t, 1:2]),
                              inc="ve")
                        em.op("vector", lambda e, t=t:
                              e.tensor_scalar_mul(lnt[:, t, 2:3], lnt[:, t, 0:1],
                                                  -1.0 / C),
                              inc="ve")
                        em.op("vector", lambda e, t=t:
                              e.tensor_scalar_mul(lnt[:, t, 3:4], lnt[:, t, 1:2],
                                                  1.0 / C),
                              inc="ve")
                        em.op("vector", lambda e, t=t:
                              e.tensor_mul(lnt[:, t, 4:5], lnt[:, t, 2:3],
                                           lnt[:, t, 2:3]),
                              inc="ve")
                        em.op("vector", lambda e, t=t:
                              e.tensor_sub(lnt[:, t, 5:6], lnt[:, t, 3:4],
                                           lnt[:, t, 4:5]),
                              inc="ve")
                        em.mark(("var", t))
                        em.waitck("scalar", ("var", t), "ve")
                        em.op("scalar", lambda e, t=t:
                              e.activation(lnt[:, t, 6:7], lnt[:, t, 5:6], AF.Ln,
                                           bias=EPS),
                              inc="sc")
                        em.op("scalar", lambda e, t=t:
                              e.activation(lnt[:, t, 7:8], lnt[:, t, 6:7], AF.Exp,
                                           scale=-0.5),
                              inc="sc")
                        em.mark(("rstd", t))

                        if t + 2 < 16:
                            em.waitck("scalar", ("rtdone", t), "ve")
                            em.dma("scalar", xst[par][:, :],
                                   xown[(t + 2) * 128:(t + 3) * 128, :],
                                   "sa%d" % par)
                            em.mark(("xst", t + 2))
                        em.waitck("vector", ("rstd", t), "sc")
                        em.op("vector", lambda e, t=t:
                              e.tensor_mul(lnt[:, t, 6:7], lnt[:, t, 2:3],
                                           lnt[:, t, 7:8]),
                              inc="ve")
                        em.op("vector", lambda e, t=t, par=par:
                              e.tensor_scalar(yacc[:, t, :], yacc[:, t, :],
                                              lnt[:, t, 7:8], lnt[:, t, 6:7],
                                              ALU.mult, ALU.add),
                              inc="ve")
                        em.mark(("x1", t))
                        em.waitck("vector", ("x1tmm", t - 2, NCC - 1), "mm")
                        em.op("vector", lambda e, t=t, par=par:
                              e.tensor_copy(x1c[par][:, :], yacc[:, t, :]),
                              inc="ve")
                        em.mark(("x1c", t))
                    emit_tr(14)
                    emit_tr(15)

                run_block(sched2)

            scopeC = contextlib.ExitStack()
            with scopeC:
                _a = scopeC.enter_context
                ph0 = _a(nc.psum_tensor("ph0", [128, 512], F32))
                ph1 = _a(nc.psum_tensor("ph1", [128, 512], F32))
                py0 = _a(nc.psum_tensor("py0", [128, C], F32))
                py1 = _a(nc.psum_tensor("py1", [128, C], F32))
                pmt = _a(nc.psum_tensor("pmt", [8, 128], BF16))
                ph = [ph0, ph1]
                py = [py0, py1]
                w1sb = [w1a, w1b]
                w2sb = [w2a, w2b]
                sqc = [sqc0, sqc1]

                def w1ap(e_, p):
                    return bass.AP(w1, e_ * C * C + p * 4 * 128 * C,
                                   [[C, 128], [128 * C, 4], [1, C]])

                def w2ap(e_, p):
                    return bass.AP(w2, e_ * C * C + p * 4 * 128 * C,
                                   [[C, 128], [128 * C, 4], [1, C]])

                def sched3():
                    em.waitck("sync", ("onrm", 1), "ve")
                    em.dma("sync", identb[:, :], ident_c.ap(), "scn")
                    em.mark("identb_ld")
                    em.dma("sync", b1_sb[:, :, :],
                           bass.AP(b1, 0, [[1, 128], [C, 8], [128, 8]]), "scn",
                           slow=True)
                    em.mark("ffn_small")
                    em.waitck("gpsimd", ("pjmm", 15), "mm")
                    em.dma("gpsimd", wg_sb[:, :, :],
                           bass.AP(wg, 0, [[8, 128], [128 * 8, NCC], [1, 8]]),
                           "swv")
                    em.dma("gpsimd", bgrow_b[:, :],
                           bass.AP(bg, 0, [[1, 1], [1, 8]]), "swh")
                    em.waitck("gpsimd", ("var", 15), "ve")
                    em.dma("gpsimd", b2r_sb[:, :],
                           bass.AP(b2, 0, [[C, 8], [1, C]]), "swh")
                    em.mark("gate_w")
                    for e_ in range(2):
                        em.dma("gpsimd", w1sb[e_][:, 0:4, :], w1ap(e_, 0),
                               "sw1%d" % e_)
                        em.dma("gpsimd", w1sb[e_][:, 4:8, :], w1ap(e_, 1),
                               "sw1%d" % e_)
                        em.mark(("w1", e_))
                        em.dma("gpsimd", w2sb[e_][:, 0:4, :], w2ap(e_, 0),
                               "sw2%d" % e_)
                        em.dma("gpsimd", w2sb[e_][:, 4:8, :], w2ap(e_, 1),
                               "sw2%d" % e_)
                        em.mark(("w2", e_))

                    em.waitck("tensor", "gate_w", "swv")
                    em.waitck("tensor", "gate_w", "swh")
                    em.waitck("tensor", "identb_ld", "scn")
                    for t in range(16):
                        par = t % 2
                        em.waitck("tensor", ("gev", t - 2), "sc")
                        for cc in range(NCC):
                            em.op("tensor", lambda e, t=t, cc=cc, par=par:
                                  e.matmul(py[par][:, 0:8],
                                           x1t[:, cc, t * 128:(t + 1) * 128],
                                           wg_sb[:, cc, :],
                                           start=(cc == 0), stop=False),
                                  inc=None)
                        em.op("tensor", lambda e, par=par:
                              e.matmul(py[par][:, 0:8], onesr[:, :], bgrow_b[:, :],
                                       start=False, stop=True),
                              inc="mm")
                        em.mark(("gmm", t))
                        em.waitck("scalar", ("gmm", t), "mm")
                        em.op("scalar", lambda e, t=t, par=par:
                              e.activation(lg[:, t, :], py[par][:, 0:8], AF.Copy),
                              inc="sc")
                        em.mark(("gev", t))
                        em.waitck("vector", ("gev", t), "sc")
                        em.op("vector", lambda e, t=t:
                              e.reduce_max(mxc[:, :], lg[:, t, :], axis=AX.X),
                              inc="ve")
                        em.op("vector", lambda e, t=t:
                              e.tensor_scalar(mstore[:, t, :], lg[:, t, :], mxc[:, :],
                                              None, ALU.is_equal),
                              inc="ve")
                        em.waitck("vector", ("mtr", t - 1), "mm")
                        em.op("vector", lambda e, t=t:
                              e.tensor_copy(mtb[:, :], mstore[:, t, :]),
                              inc="ve")
                        em.mark(("mtb", t))
                        em.waitck("tensor", ("mtb", t), "ve")
                        em.waitck("tensor", "identb_ld", "scn")
                        em.waitck("tensor", ("mtr_ev", t - 1), "sc")
                        em.op("tensor", lambda e: e.transpose(pmt[:, :], mtb[:, :],
                                                              identb[:, :]),
                              inc="mm")
                        em.mark(("mtr", t))
                        em.waitck("scalar", ("mtr", t), "mm")
                        em.op("scalar", lambda e, t=t:
                              e.activation(maskT[:, t, :], pmt[:, :], AF.Copy),
                              inc="sc")
                        em.mark(("mtr_ev", t))
                    em.mark("maskr")

                    def emit_b2pass():
                        for t in range(16):
                            par = t % 2
                            if t < 2:
                                em.waitck("tensor", ("gev", 14 + t), "sc")
                            else:
                                em.waitck("tensor", ("b2add", t - 2), "ve")
                            em.waitck("tensor", ("mtr_ev", t), "sc")
                            for h in range(2):
                                em.op("tensor", lambda e, t=t, h=h, par=par:
                                      e.matmul(py[par][:, h * 512:(h + 1) * 512],
                                               maskT[:, t, :],
                                               b2r_sb[:, h * 512:(h + 1) * 512],
                                               start=True, stop=True),
                                      inc=("mm" if h == 1 else None))
                            em.mark(("b2mm", t))
                            em.waitck("vector", ("b2mm", t), "mm")
                            em.op("vector", lambda e, t=t, par=par:
                                  e.tensor_add(yacc[:, t, :], yacc[:, t, :],
                                               py[par][:, :]),
                                  inc="ve")
                            em.mark(("b2add", t))

                    def ln2_stage1(t):
                        em.op("vector", lambda e, t=t:
                              e.scalar_tensor_tensor(sqc[t % 2][:, :], yacc[:, t, :],
                                                     1.0, yacc[:, t, :],
                                                     ALU.mult, ALU.mult,
                                                     accum_out=lnt2[:, t, 1:2]),
                              inc="ve")
                        em.op("vector", lambda e, t=t:
                              e.tensor_scalar_mul(lnt2[:, t, 2:3], lnt2[:, t, 0:1],
                                                  -1.0 / C),
                              inc="ve")
                        em.op("vector", lambda e, t=t:
                              e.tensor_scalar_mul(lnt2[:, t, 3:4], lnt2[:, t, 1:2],
                                                  1.0 / C),
                              inc="ve")
                        em.op("vector", lambda e, t=t:
                              e.tensor_mul(lnt2[:, t, 4:5], lnt2[:, t, 2:3],
                                           lnt2[:, t, 2:3]),
                              inc="ve")
                        em.op("vector", lambda e, t=t:
                              e.tensor_sub(lnt2[:, t, 5:6], lnt2[:, t, 3:4],
                                           lnt2[:, t, 4:5]),
                              inc="ve")
                        em.mark(("var2", t))
                        em.waitck("scalar", ("var2", t), "ve")
                        em.op("scalar", lambda e, t=t:
                              e.activation(lnt2[:, t, 6:7], lnt2[:, t, 5:6], AF.Ln,
                                           bias=EPS),
                              inc="sc")
                        em.op("scalar", lambda e, t=t:
                              e.activation(lnt2[:, t, 7:8], lnt2[:, t, 6:7], AF.Exp,
                                           scale=-0.5),
                              inc="sc")
                        em.mark(("rstd2", t))

                    def ln2_finish(t):
                        em.waitck("vector", ("rstd2", t), "sc")
                        em.op("vector", lambda e, t=t:
                              e.tensor_mul(lnt2[:, t, 6:7], lnt2[:, t, 2:3],
                                           lnt2[:, t, 7:8]),
                              inc="ve")
                        em.op("vector", lambda e, t=t:
                              e.tensor_scalar(yacc[:, t, :], yacc[:, t, :],
                                              lnt2[:, t, 7:8], lnt2[:, t, 6:7],
                                              ALU.mult, ALU.add),
                              inc="ve")
                        em.mark(("ofin", t))
                        em.waitck("scalar", ("ofin", t), "ve")
                        em.dma("scalar", out[t * 128:(t + 1) * 128, :],
                               yacc[:, t, :], "so%d" % (t % 8))
                        em.mark(("ostd", t))

                    em.waitck("scalar", "ffn_small", "scn")
                    for e_ in range(8):
                        em.waitck("tensor", ("w1", e_), "sw1%d" % (e_ % 2))
                        em.waitck("tensor", ("w1", e_), "sw1%db" % (e_ % 2))
                        for th in range(2):
                            for ic in range(NCC):
                                for tq in range(2):
                                    idx = ic * 2 + tq
                                    reg = idx % 2
                                    if idx < 2:
                                        if th == 1:
                                            em.waitck("tensor",
                                                      ("hev", e_ * 2, 14 + idx), "sc")
                                        else:
                                            em.waitck("tensor",
                                                      ("hev", (e_ - 1) * 2 + 1,
                                                       14 + idx), "sc")
                                    else:
                                        em.waitck("tensor",
                                                  ("hev", e_ * 2 + th, idx - 2), "sc")
                                    for cc in range(NCC):
                                        em.op("tensor",
                                              lambda e, ic=ic, tq=tq, cc=cc, th=th, reg=reg, e_=e_:
                                              e.matmul(ph[reg][:, :],
                                                       w1sb[e_ % 2][:, cc, ic * 128:(ic + 1) * 128],
                                                       x1t[:, cc, th * 1024 + tq * 512:
                                                           th * 1024 + (tq + 1) * 512],
                                                       start=(cc == 0), stop=(cc == 7)),
                                              inc=("mm" if cc == 7 else None))
                                    em.mark(("hmm", e_ * 2 + th, idx))
                                    em.waitck("scalar", ("hmm", e_ * 2 + th, idx), "mm")
                                    em.op("scalar", lambda e, ic=ic, tq=tq, reg=reg, e_=e_:
                                          e.activation(hsb[:, ic, tq * 512:(tq + 1) * 512],
                                                       ph[reg][:, :], AF.Gelu,
                                                       bias=b1_sb[:, e_, ic:ic + 1]),
                                          inc="sc")
                                    em.mark(("hev", e_ * 2 + th, idx))
                            if th == 1:
                                em.mark(("hdone", e_))
                                if e_ + 2 < 8:
                                    em.waitck("gpsimd", ("hdone", e_), "mm")
                                    em.dma("gpsimd", w1sb[e_ % 2][:, 0:4, :],
                                           w1ap(e_ + 2, 0), "sw1%d" % (e_ % 2))
                                    em.dma("gpsimd", w1sb[e_ % 2][:, 4:8, :],
                                           w1ap(e_ + 2, 1), "sw1%d" % (e_ % 2))
                                    em.mark(("w1", e_ + 2))

                            if e_ == 0 and th == 0:
                                emit_b2pass()

                            em.waitck("tensor", ("hev", e_ * 2 + th, 15), "sc")
                            em.waitck("tensor", ("w2", e_), "sw2%d" % (e_ % 2))
                            em.waitck("tensor", ("w2", e_), "sw2%db" % (e_ % 2))
                            for tt in range(8):
                                t = th * 8 + tt
                                par = tt % 2
                                if tt < 2:
                                    if e_ == 0 and th == 0:
                                        em.waitck("tensor", ("b2add", 14 + tt), "ve")
                                    else:
                                        em.waitck("tensor",
                                                  ("yadd", e_ * 2 + th - 1, 6 + tt),
                                                  "ve")
                                else:
                                    em.waitck("tensor",
                                              ("yadd", e_ * 2 + th, tt - 2), "ve")
                                for h in range(2):
                                    for ic in range(NCC):
                                        em.op("tensor",
                                              lambda e, tt=tt, h=h, ic=ic, par=par, e_=e_:
                                              e.matmul(py[par][:, h * 512:(h + 1) * 512],
                                                       hsb[:, ic, tt * 128:(tt + 1) * 128],
                                                       w2sb[e_ % 2][:, ic, h * 512:(h + 1) * 512],
                                                       start=(ic == 0), stop=(ic == 7)),
                                              inc=("mm" if (ic == 7 and h == 1) else None))
                                em.mark(("ymm", e_ * 2 + th, tt))
                                em.waitck("vector", ("ymm", e_ * 2 + th, tt), "mm")
                                if e_ == 7:
                                    em.op("vector", lambda e, t=t, par=par, e_=e_:
                                          e.scalar_tensor_tensor(
                                              yacc[:, t, :], py[par][:, :],
                                              mstore[:, t, e_:e_ + 1], yacc[:, t, :],
                                              ALU.mult, ALU.add,
                                              accum_out=lnt2[:, t, 0:1]),
                                          inc="ve")
                                else:
                                    em.op("vector", lambda e, t=t, par=par, e_=e_:
                                          e.scalar_tensor_tensor(
                                              yacc[:, t, :], py[par][:, :],
                                              mstore[:, t, e_:e_ + 1], yacc[:, t, :],
                                              ALU.mult, ALU.add),
                                          inc="ve")
                                em.mark(("yadd", e_ * 2 + th, tt))
                                if e_ == 7:
                                    ln2_stage1(t)
                                    if t >= 1:
                                        ln2_finish(t - 1)
                            if th == 1:
                                em.mark(("ydone", e_))
                                if e_ + 2 < 8:
                                    em.waitck("gpsimd", ("ydone", e_), "mm")
                                    em.dma("gpsimd", w2sb[e_ % 2][:, 0:4, :],
                                           w2ap(e_ + 2, 0), "sw2%d" % (e_ % 2))
                                    em.dma("gpsimd", w2sb[e_ % 2][:, 4:8, :],
                                           w2ap(e_ + 2, 1), "sw2%d" % (e_ % 2))
                                    em.mark(("w2", e_ + 2))

                    ln2_finish(15)
                    for _k in range(8):
                        em.wait("scalar", "so%d" % _k, 16 * 2)

                run_block(sched3)

    return nc


_NC_CACHE = {}


def kernel(L0_attn_x, L0_attn_w_attn, L0_attn_b_attn, L0_attn_w_proj,
           L0_attn_b_proj, L0_ffn_Wg, L0_ffn_bg, L0_ffn_W1, L0_ffn_b1,
           L0_ffn_W2, L0_ffn_b2):
    x = np.ascontiguousarray(np.asarray(L0_attn_x, dtype=np.float32))
    x2 = x.reshape(B * T, C)
    shared = {
        "wattn": np.ascontiguousarray(np.asarray(L0_attn_w_attn, np.float32)),
        "battn": np.ascontiguousarray(np.asarray(L0_attn_b_attn, np.float32)),
        "wproj": np.ascontiguousarray(np.asarray(L0_attn_w_proj, np.float32)),
        "bproj": np.ascontiguousarray(np.asarray(L0_attn_b_proj, np.float32)),
        "wg": np.ascontiguousarray(np.asarray(L0_ffn_Wg, np.float32)),
        "bg": np.ascontiguousarray(np.asarray(L0_ffn_bg, np.float32)),
        "w1": np.ascontiguousarray(np.asarray(L0_ffn_W1, np.float32)),
        "b1": np.ascontiguousarray(np.asarray(L0_ffn_b1, np.float32)),
        "w2": np.ascontiguousarray(np.asarray(L0_ffn_W2, np.float32)),
        "b2": np.ascontiguousarray(np.asarray(L0_ffn_b2, np.float32)),
    }

    xTp = [np.ascontiguousarray(x2[0:T].T),
           np.ascontiguousarray(x2[T:2 * T].T)]
    tri = np.triu(np.ones((128, 128), np.float32))
    in_maps = []
    for i in range(NCORE):
        blocks = _blocks_for_core(i)
        bA = i % 2
        view_batches = [bA, 1 - bA]
        rows = []
        for b in view_batches:
            for m in range(8):
                blk = blocks[m]
                rows.append(np.arange(blk * 128, (blk + 1) * 128) + b * T)
        rows = np.concatenate(rows)
        xo = np.ascontiguousarray(x2[rows])

        qm = np.zeros((64, 128, 128), np.float32)
        for k in range(64):
            a, r = k // 8, k % 8
            sig = i if a % 2 == 0 else 7 - i
            if sig > r:
                qm[k] = 1.0
            elif sig == r:
                qm[k] = tri
        qm = qm.reshape(64 * 128, 128).astype(BF)

        m = dict(shared)
        m["xT"] = xTp[bA]
        m["xown"] = xo
        m["xownT"] = np.ascontiguousarray(xo.T)
        m["qmask"] = np.ascontiguousarray(qm)
        in_maps.append(m)

    if "full" not in _NC_CACHE:
        _NC_CACHE["full"] = build()
    nc = _NC_CACHE["full"]

    res = run_bass_kernel_spmd(nc, in_maps, core_ids=list(range(NCORE)))

    outf = np.empty((B, T, C), np.float32)
    for i in range(NCORE):
        o = res.results[i]["out"]
        blocks = _blocks_for_core(i)
        bA = i % 2
        view_batches = [bA, 1 - bA]
        for vb, b in enumerate(view_batches):
            for m in range(8):
                blk = blocks[m]
                outf[b, blk * 128:(blk + 1) * 128, :] = \
                    o[(vb * 8 + m) * 128:(vb * 8 + m + 1) * 128, :]
    return outf


# version 17
# speedup vs baseline: 1.0464x; 1.0464x over previous
import contextlib

import numpy as np
import ml_dtypes
import concourse.bass as bass
import concourse.mybir as mybir
from concourse.bass_utils import run_bass_kernel_spmd

F32 = mybir.dt.float32
BF16 = mybir.dt.bfloat16
AF = mybir.ActivationFunctionType
ALU = mybir.AluOpType
AX = mybir.AxisListType
BF = ml_dtypes.bfloat16

B, T, C = 2, 8192, 1024
NCORE = 8
TOK = 2048
NCC = 8
NRING = 8
RK = 1024
ATT_SCALE = 1.0 / 32.0
ESH = 8.0
EPS = 1e-5

SEMS = ("mm", "sc", "ve",
        "sxc0", "sxc1", "sxc2", "sxc3",
        "swv",
        "sxo",
        "sxt",
        "scn",
        "st00", "st01", "st02", "st03",
        "st10", "st11", "st12", "st13",
        "skv", "skv2", "skv3", "skv4",
        "swh",
        "swh2", "swh3",
        "cc",
        "klg", "klg2", "klg3", "klg4",
        "ssp", "ssp2",
        "son", "son2", "son3", "son4",
        "sa0", "sa1",
        "sw10", "sw11", "sw20", "sw21",
        "sw10b", "sw11b", "sw20b", "sw21b",
        "so0", "so1", "so2", "so3",
        "so4", "so5", "so6", "so7")


def _blocks_for_core(i):
    return [8 * m + (i if m % 2 == 0 else 7 - i) for m in range(8)]


def _has_neg(key):
    if isinstance(key, tuple):
        return any(isinstance(k, int) and k < 0 for k in key)
    return False


class Emitter:

    def __init__(self, nc, sems):
        self.nc = nc
        self.sems = sems

    def begin(self, lab, eng):
        self.lab = lab
        self.eng = eng
        self.cnt = {s: 0 for s in SEMS}
        self.ck = {}
        self.lastwait = {}

    def wait(self, engine, sem, val):
        if val is None or val <= 0:
            return
        if self.lab == engine:
            if self.lastwait.get(sem, -1) >= val:
                return
            self.lastwait[sem] = val
            self.eng.wait_ge(self.sems[sem], val)

    def waitck(self, engine, key, sem):
        if key not in self.ck:
            if _has_neg(key):
                return
            raise KeyError(f"waitck: missing checkpoint {key!r}")
        self.wait(engine, sem, self.ck[key][sem])

    def waitall(self, engine, sem):
        self.wait(engine, sem, self.cnt[sem])

    def op(self, engine, build, inc=None, incn=1):
        if engine == "vector":
            self.wait(engine, "ve", self.cnt["ve"])
        elif engine == "scalar" and inc == "sc":
            self.wait(engine, "sc", self.cnt["sc"])
        if self.lab == engine:
            ins = build(self.eng)
            if inc is not None:
                ins.then_inc(self.sems[inc], incn)
        if inc is not None:
            self.cnt[inc] += incn

    def dma(self, engine, out, in_, inc, transpose=False, slow=False):
        self.wait(engine, inc, self.cnt[inc])
        if slow:
            self.op(engine,
                    lambda e: e.dma_start(out, in_,
                                          allow_slow_non_contiguous=True),
                    inc=inc, incn=16)
        else:
            self.op(engine, lambda e: e.dma_start(out, in_, transpose=transpose),
                    inc=inc, incn=16)

    def mark(self, key):
        self.ck[key] = dict(self.cnt)

    def at(self, key, sem):
        return self.ck[key][sem] if key in self.ck else 0


def build():
    nc = bass.Bass("TRN2", target_bir_lowering=False)

    xT = nc.dram_tensor("xT", [C, T], F32, kind="ExternalInput")
    xown = nc.dram_tensor("xown", [TOK, C], F32, kind="ExternalInput")
    xownT = nc.dram_tensor("xownT", [C, TOK], F32, kind="ExternalInput")
    wattn = nc.dram_tensor("wattn", [C, 3 * C], F32, kind="ExternalInput")
    battn = nc.dram_tensor("battn", [3 * C], F32, kind="ExternalInput")
    wproj = nc.dram_tensor("wproj", [C, C], F32, kind="ExternalInput")
    bproj = nc.dram_tensor("bproj", [C], F32, kind="ExternalInput")
    wg = nc.dram_tensor("wg", [C, 8], F32, kind="ExternalInput")
    bg = nc.dram_tensor("bg", [8], F32, kind="ExternalInput")
    w1 = nc.dram_tensor("w1", [8, C, C], F32, kind="ExternalInput")
    b1 = nc.dram_tensor("b1", [8, C], F32, kind="ExternalInput")
    w2 = nc.dram_tensor("w2", [8, C, C], F32, kind="ExternalInput")
    b2 = nc.dram_tensor("b2", [8, C], F32, kind="ExternalInput")
    qmask = nc.dram_tensor("qmask", [64 * 128, 128], BF16, kind="ExternalInput")

    out = nc.dram_tensor("out", [TOK, C], F32, kind="ExternalOutput")

    wbp = nc.dram_tensor("wbp", [C, C], BF16)
    wb1 = nc.dram_tensor("wb1", [8, C, C], BF16)
    wb2 = nc.dram_tensor("wb2", [8, C, C], BF16)
    wbg = nc.dram_tensor("wbg", [C, 8], BF16)
    KVM = 128 * NCC * RK
    kvsend = nc.dram_tensor("kvsend", [NRING, 2, 128, NCC * RK], BF16)
    kvgath = nc.dram_tensor("kvgath", [NRING, 2, 2, 128, NCC * RK], BF16)

    consts_np = np.zeros((128, 2), np.float32)
    consts_np[:, 0] = -ESH
    consts_np[:, 1] = EPS
    consts_c = nc.inline_tensor(consts_np, "fconsts_c")
    ones_r = nc.inline_tensor(np.ones((1, 128), BF), "onesr_c")
    ones_sq = nc.inline_tensor(np.ones((128, 128), BF), "ones128_c")
    ident_c = nc.inline_tensor(np.eye(128).astype(BF), "ident_c")
    ehot_np = np.zeros((8, 8, 128), np.float32)
    for e_ in range(8):
        ehot_np[e_, e_, :] = 1.0
    ehot_c = nc.inline_tensor(ehot_np.transpose(1, 0, 2).astype(BF), "ehot_c")

    ctx = contextlib.ExitStack()
    sems = {s: ctx.enter_context(nc.semaphore(s)) for s in SEMS}
    em = Emitter(nc, sems)

    chunks = [(b, a) for b in range(B) for a in range(NRING)]

    persist_ck = {}
    persist_cnt = {s: 0 for s in SEMS}

    def run_block(schedule_fn):
        saved = {}
        with nc.Block(no_gpsimd_drain=True) as blk:
            for lab, reg in (("tensor", blk.tensor), ("scalar", blk.scalar),
                             ("vector", blk.vector), ("sync", blk.sync),
                             ("gpsimd", blk.gpsimd)):
                def f(eng, lab=lab):
                    em.begin(lab, eng)
                    em.ck.update(persist_ck)
                    em.cnt.update(persist_cnt)
                    schedule_fn()
                    saved["ck"] = dict(em.ck)
                    saved["cnt"] = dict(em.cnt)
                reg(f)
        persist_ck.update(saved["ck"])
        persist_cnt.update(saved["cnt"])

    with ctx:
        with nc.Block(no_gpsimd_drain=True) as _blk0:
            def _clr(eng):
                for _s in SEMS:
                    eng.sem_clear(sems[_s])
            _blk0.sync(_clr)
        onesr = ctx.enter_context(nc.sbuf_tensor("onesr", [1, 128], BF16))
        cst_sb = ctx.enter_context(nc.sbuf_tensor("cst_sb", [128, 2], F32))
        nc.const_aps.register(F32, -ESH, cst_sb[:, 0:1])
        nc.const_aps.register(F32, EPS, cst_sb[:, 1:2])
        div_handles = {}
        if True:
            scopeA = contextlib.ExitStack()
            with scopeA:
                Qp = scopeA.enter_context(nc.sbuf_tensor("Qp", [128, NCC, TOK], BF16))
                wk_sb = scopeA.enter_context(nc.sbuf_tensor("wk_sb", [128, NCC, C], BF16))
                wv_sb = scopeA.enter_context(nc.sbuf_tensor("wv_sb", [128, NCC, C], BF16))
                masks = scopeA.enter_context(nc.sbuf_tensor("masks", [128, 64, 128], BF16))
                ones128 = scopeA.enter_context(nc.sbuf_tensor("ones128", [128, 128], BF16))
                bq_sb = scopeA.enter_context(nc.sbuf_tensor("bq_sb", [128, NCC], F32))
                bk_sb = scopeA.enter_context(nc.sbuf_tensor("bk_sb", [128, NCC], F32))
                bvrow = scopeA.enter_context(nc.sbuf_tensor("bvrow", [1, C], F32))
                bvrow_b = scopeA.enter_context(nc.sbuf_tensor("bvrow_b", [1, C], BF16))
                bvb = scopeA.enter_context(nc.sbuf_tensor("bvb", [128, C], F32))

                scope0 = contextlib.ExitStack()
                with scope0:
                    wq_sb = scope0.enter_context(nc.sbuf_tensor("wq_sb", [128, NCC, C], BF16))
                    xto = scope0.enter_context(nc.sbuf_tensor("xto", [128, NCC, TOK], BF16))
                    qps0 = scope0.enter_context(nc.psum_tensor("qps0", [128, 512], F32))
                    qps1 = scope0.enter_context(nc.psum_tensor("qps1", [128, 512], F32))
                    bvps = scope0.enter_context(nc.psum_tensor("bvps", [128, C], F32))
                    qps = [qps0, qps1]

                    def sched0():
                        for q in range(4):
                            em.dma("gpsimd", xto[:, :, q * 512:(q + 1) * 512],
                                   bass.AP(xownT, q * 512,
                                           [[TOK, 128], [128 * TOK, NCC],
                                            [1, 512]]),
                                   "st0%d" % q)
                            em.mark(("xtp", q))
                        for name, sb, co0 in (("wq", wq_sb, 0),
                                              ("wk", wk_sb, C)):
                            em.dma("gpsimd", sb[:, :, :],
                                   bass.AP(wattn, co0,
                                           [[3 * C, 128], [128 * 3 * C, NCC], [1, C]]),
                                   "swv")
                        em.mark("wqkv")

                        em.dma("sync", cst_sb[:, :], consts_c.ap(), "scn")
                        em.dma("sync", onesr[:, :], ones_r.ap(), "scn")
                        em.dma("sync", ones128[:, :], ones_sq.ap(), "scn")
                        em.dma("sync", masks[:, :, :],
                               bass.AP(qmask, 0, [[128, 128], [128 * 128, 64], [1, 128]]),
                               "scn")
                        em.dma("sync", bq_sb[:, :],
                               bass.AP(battn, 0, [[1, 128], [128, NCC]]), "scn",
                               slow=True)
                        em.dma("sync", bk_sb[:, :],
                               bass.AP(battn, C, [[1, 128], [128, NCC]]), "scn",
                               slow=True)
                        em.dma("sync", bvrow[:, :],
                               bass.AP(battn, 2 * C, [[1, 1], [1, C]]), "scn")
                        em.mark("consts")

                        em.waitck("scalar", "consts", "scn")
                        em.op("scalar", lambda e: e.copy(bvrow_b[:, :], bvrow[:, :]),
                              inc="sc")
                        em.mark("bvrow_b")

                        em.waitck("tensor", "wqkv", "swv")
                        em.waitck("tensor", "consts", "scn")
                        for co in range(NCC):
                            for qs in range(TOK // 512):
                                idx = co * 4 + qs
                                r = idx % 2
                                em.waitck("tensor", ("xtp", qs), "st0%d" % qs)
                                em.waitck("tensor", ("qev", idx - 2), "sc")
                                for ci in range(NCC):
                                    em.op("tensor", lambda e, co=co, qs=qs, ci=ci, r=r:
                                          e.matmul(qps[r][:, :],
                                                   wq_sb[:, ci, co * 128:(co + 1) * 128],
                                                   xto[:, ci, qs * 512:(qs + 1) * 512],
                                                   start=(ci == 0), stop=(ci == 7)),
                                          inc=("mm" if ci == 7 else None))
                                em.mark(("qmm", idx))
                                em.waitck("scalar", ("qmm", idx), "mm")
                                em.op("scalar", lambda e, co=co, qs=qs, r=r:
                                      e.activation(Qp[:, co, qs * 512:(qs + 1) * 512],
                                                   qps[r][:, :], AF.Identity,
                                                   bias=bq_sb[:, co:co + 1]),
                                      inc="sc")
                                em.mark(("qev", idx))
                        em.waitck("tensor", "bvrow_b", "sc")
                        for h in range(2):
                            em.op("tensor", lambda e, h=h:
                                  e.matmul(bvps[:, h * 512:(h + 1) * 512], onesr[:, :],
                                           bvrow_b[:, h * 512:(h + 1) * 512],
                                           start=True, stop=True),
                                  inc=("mm" if h == 1 else None))
                        em.mark("bvmm")
                        em.waitck("scalar", "bvmm", "mm")
                        em.op("scalar", lambda e: e.activation(bvb[:, :], bvps[:, :],
                                                               AF.Copy), inc="sc")
                        em.mark("bvb")

                    run_block(sched0)

                scope1 = contextlib.ExitStack()
                with scope1:
                    xt0 = scope1.enter_context(nc.sbuf_tensor("xt0", [128, NCC, RK], BF16))
                    xt1 = scope1.enter_context(nc.sbuf_tensor("xt1", [128, NCC, RK], BF16))
                    Ksb = scope1.enter_context(nc.sbuf_tensor("Ksb", [128, NCC, RK], BF16))
                    Vsb = scope1.enter_context(nc.sbuf_tensor("Vsb", [128, NCC, RK], BF16))
                    Psb = scope1.enter_context(nc.sbuf_tensor("Psb", [128, NCC, RK], BF16))
                    Op = scope1.enter_context(nc.sbuf_tensor("Op", [128, NCC, 1024], F32))
                    lsb = scope1.enter_context(nc.sbuf_tensor("lsb", [128, 1024], F32))
                    pK0 = scope1.enter_context(nc.psum_tensor("pK0", [128, RK], F32))
                    pK1 = scope1.enter_context(nc.psum_tensor("pK1", [128, RK], F32))
                    pV0 = scope1.enter_context(nc.psum_tensor("pV0", [128, RK], F32))
                    pV1 = scope1.enter_context(nc.psum_tensor("pV1", [128, RK], F32))
                    xts = [xt0, xt1]
                    pK = [pK0, pK1]
                    pV = [pV0, pV1]
                    rkv = scope1.enter_context(nc.sync.register("rkv"))
                    rtmp = scope1.enter_context(nc.sync.register("rtmp"))

                    AT = nc.alloc_sbuf_tensor_at
                    adr = lambda h: nc.lookup_mloc(h).addr
                    on0m = AT("on0m", [128, NCC, 1024], BF16, offset=adr(wv_sb))
                    on1m = AT("on1m", [128, NCC, 1024], BF16, offset=adr(masks))
                    wpm = AT("wpm", [128, NCC, C], BF16, offset=adr(wk_sb))
                    xst0 = AT("xst0", [128, C], F32, offset=adr(bq_sb))
                    xst1 = AT("xst1", [128, C], F32, offset=adr(bvrow_b))
                    bpjrow_b = AT("bpjrow_b", [1, C], BF16,
                                  offset=adr(bvrow_b) + 4096)
                    assert adr(xt1) == adr(xt0) + 16384
                    assert adr(Ksb) == adr(xt0) + 2 * 16384
                    assert adr(Vsb) == adr(xt0) + 3 * 16384
                    yacc_m = AT("yacc", [128, 16, C], F32, offset=adr(xt0))
                    sqb0 = AT("sqb0", [128, C], F32, offset=adr(Psb))
                    lnt = AT("lnt", [128, 16, 8], F32, offset=adr(Psb) + 4096)
                    identb2 = AT("identb2", [128, 128], BF16,
                                 offset=adr(Psb) + 4608)
                    x1c0 = AT("x1c0", [128, C], BF16, offset=adr(Psb) + 4864)
                    x1c1 = AT("x1c1", [128, C], BF16, offset=adr(Psb) + 6912)
                    w1a = AT("w1a", [128, NCC, C], BF16, offset=adr(wv_sb))
                    w1b = AT("w1b", [128, NCC, C], BF16, offset=adr(wk_sb))
                    w2a = AT("w2a", [128, NCC, C], BF16, offset=adr(masks))
                    w2b = AT("w2b", [128, NCC, C], BF16,
                             offset=adr(Psb) + 8960)
                    hsb = AT("hsb", [128, NCC, 1024], BF16,
                             offset=adr(Psb) + 8960 + 16384)
                    sqc0 = AT("sqc0", [128, C], F32,
                              offset=adr(Psb) + 8960 + 2 * 16384)
                    sqc1 = AT("sqc1", [128, C], F32,
                              offset=adr(Psb) + 8960 + 2 * 16384 + 4096)
                    lnt2 = AT("lnt2", [128, 16, 8], F32,
                              offset=adr(Psb) + 8960 + 2 * 16384 + 2 * 4096)
                    _sm = adr(Psb) + 8960 + 2 * 16384 + 2 * 4096 + 512
                    wg_sb = AT("wg_sb", [128, NCC, 8], BF16, offset=_sm)
                    bgrow_b = AT("bgrow_b", [1, 8], BF16, offset=_sm + 128)
                    b1_sb = AT("b1_sb", [128, 8, 8], F32, offset=_sm + 160)
                    identb = AT("identb", [128, 128], BF16, offset=_sm + 416)
                    lg = AT("lg", [128, 16, 8], F32, offset=_sm + 672)
                    mstore = AT("mstore", [128, 16, 8], F32, offset=_sm + 1184)
                    mxc = AT("mxc", [128, 1], F32, offset=_sm + 1696)
                    mtb = AT("mtb", [128, 8], BF16, offset=_sm + 1728)
                    maskT = AT("maskT", [8, 16, 128], BF16,
                               offset=adr(Psb) + 4864)
                    b2r_sb = AT("b2r_sb", [8, C], BF16, offset=adr(Psb))
                    assert _sm + 1760 <= adr(lsb) + 4096
                    nc._b2_handles = (on0m, on1m, wpm, xst0, xst1, bpjrow_b,
                                      yacc_m, sqb0, lnt, identb2, x1c0, x1c1,
                                      w1a, w1b, w2a, w2b, hsb, sqc0, sqc1,
                                      lnt2, wg_sb, bgrow_b, b1_sb, identb, lg,
                                      mstore, mxc, mtb, maskT, b2r_sb)

                    def sched1():
                        i_batchA = chunks[0][0]

                        def xt_prefetch(n):
                            b, a = chunks[n]
                            if n >= 2:
                                em.waitck("gpsimd", ("kvmm", n - 2), "mm")
                            for p in range(2):
                                em.dma("gpsimd",
                                       xts[n % 2][:, p * 4:(p + 1) * 4, :],
                                       bass.AP(xT, a * RK + p * 4 * 128 * T,
                                               [[T, 128], [128 * T, 4],
                                                [1, RK]]),
                                       "st%d%d" % (n % 2, p))
                            em.mark(("xt", n))

                        def emit_cc(a):
                            for _s in ("skv", "skv2", "skv3", "skv4"):
                                em.waitck("gpsimd", ("kvsp", a), _s)
                            em.op("gpsimd", lambda e, a=a:
                                  e.collective_compute(
                                      "AllGather", ALU.bypass,
                                      replica_groups=[[0, 1], [2, 3], [4, 5], [6, 7]],
                                      ins=[kvsend[a, :, :, :].opt()],
                                      outs=[kvgath[a, :, :, :, :].opt()]),
                                  inc="cc")
                            em.mark(("ccd", a))

                        def kld_load(ring, dstK, dstV):
                            npc = 2 if ring < 2 else 1
                            w = NCC // npc
                            for kv, dstb in ((0, dstK), (1, dstV)):
                                if ring == 0:
                                    em.waitck("sync", ("kvmm", 6 + kv), "mm")
                                for p in range(npc):
                                    if npc == 2:
                                        s = (("klg", "klg2") if kv == 0
                                             else ("klg3", "klg4"))[p]
                                    else:
                                        s = (("klg", "klg2") if kv == 0
                                             else ("klg3", "klg4"))[ring % 2]
                                    em.wait("sync", s, em.cnt[s])
                                    em.op("sync",
                                          lambda e, ring=ring, kv=kv, p=p,
                                          dstb=dstb, w=w:
                                          (e.reg_alu(rkv, rtmp,
                                                     (ring * 4 + kv) * KVM
                                                     + p * w * RK, ALU.add),
                                           e.dma_start(
                                               dstb[:, p * w:(p + 1) * w, :],
                                               bass.AP(kvgath, rkv,
                                                       [[NCC * RK, 128],
                                                        [1, w * RK]]))
                                           )[1],
                                          inc=s, incn=16)
                                em.mark(("kldK" if kv == 0 else "kldV", ring))

                        xt_prefetch(0)
                        em.dma("gpsimd", wv_sb[:, :, :],
                               bass.AP(wattn, 2 * C,
                                       [[3 * C, 128], [128 * 3 * C, NCC],
                                        [1, C]]),
                               "swv")
                        em.mark("wvld")
                        xt_prefetch(1)

                        em.waitck("tensor", "consts", "scn")
                        em.waitck("vector", "consts", "scn")
                        em.waitck("scalar", "consts", "scn")

                        def g_slotoff(e):
                            pid = nc.partition_id(engines=[mybir.EngineType.SP])
                            e.reg_alu(rtmp, pid, 1, ALU.bitwise_and)
                            e.reg_alu(rtmp, rtmp, 1, ALU.bitwise_xor)
                            return e.reg_alu(rtmp, rtmp, 2 * KVM, ALU.mult)
                        em.op("sync", g_slotoff, inc=None)

                        for n, (b, a) in enumerate(chunks):
                            p2 = n % 2
                            qoff = a * 128
                            Nq = 1024 - qoff
                            nqs = (Nq + 511) // 512
                            batchA = n < NRING

                            if batchA:
                                for _l in range(2):
                                    em.waitck("tensor", ("xt", n),
                                              "st%d%d" % (p2, _l))
                                em.waitck("tensor", ("exps", n - 1), "sc")
                                for cc in range(NCC):
                                    em.waitck("tensor", ("kev", n, cc - 2), "sc")
                                    for h in range(2):
                                        for ci in range(NCC):
                                            em.op("tensor", lambda e, cc=cc, h=h, ci=ci, p2=p2:
                                                  e.matmul(pK[cc % 2][:, h * 512:(h + 1) * 512],
                                                           wk_sb[:, ci, cc * 128:(cc + 1) * 128],
                                                           xts[p2][:, ci, h * 512:(h + 1) * 512],
                                                           start=(ci == 0), stop=(ci == 7)),
                                                  inc=("mm" if (h == 1 and ci == 7) else None))
                                    em.mark(("kmm", n, cc))
                                    em.waitck("scalar", ("kmm", n, cc), "mm")
                                    if cc == 0:
                                        for _s in ("skv", "skv2", "skv3",
                                                    "skv4"):
                                            em.waitck("scalar",
                                                      ("kvsp", n - 1), _s)
                                    em.op("scalar", lambda e, cc=cc:
                                          e.activation(Ksb[:, cc, :], pK[cc % 2][:, :],
                                                       AF.Identity, bias=bk_sb[:, cc:cc + 1]),
                                          inc="sc")
                                    em.mark(("kev", n, cc))
                                    if cc in (3, 7):
                                        em.waitck("sync", ("kev", n, cc), "sc")
                                        em.dma("sync",
                                               kvsend[a, 0, :, (cc - 3) * 1024:
                                                      (cc + 1) * 1024],
                                               Ksb[:, cc - 3:cc + 1, :],
                                               "skv" if cc == 3 else "skv3")

                                em.waitck("tensor", "wvld", "swv")
                                em.waitck("tensor", ("avacc", n - 1), "ve")
                                em.waitck("vector", "bvb", "sc")
                                for kb in range(NCC):
                                    em.waitck("tensor", ("vev", n, kb - 2), "ve")
                                    for h in range(2):
                                        for ci in range(NCC):
                                            em.op("tensor", lambda e, kb=kb, h=h, ci=ci, p2=p2:
                                                  e.matmul(pV[kb % 2][:, h * 512:(h + 1) * 512],
                                                           xts[p2][:, ci, kb * 128:(kb + 1) * 128],
                                                           wv_sb[:, ci, h * 512:(h + 1) * 512],
                                                           start=(ci == 0), stop=(ci == 7)),
                                                  inc=("mm" if (h == 1 and ci == 7) else None))
                                    em.mark(("vmm", n, kb))
                                    em.waitck("vector", ("vmm", n, kb), "mm")
                                    if kb == 0:
                                        for _s in ("skv", "skv2", "skv3",
                                                    "skv4"):
                                            em.waitck("vector",
                                                      ("kvsp", n - 1), _s)
                                    em.op("vector", lambda e, kb=kb:
                                          e.tensor_tensor(Vsb[:, kb, :], pV[kb % 2][:, :],
                                                          bvb[:, :], ALU.add),
                                          inc="ve")
                                    em.mark(("vev", n, kb))
                                    if kb in (3, 7):
                                        em.waitck("sync", ("vev", n, kb), "ve")
                                        em.dma("sync",
                                               kvsend[a, 1, :, (kb - 3) * 1024:
                                                      (kb + 1) * 1024],
                                               Vsb[:, kb - 3:kb + 1, :],
                                               "skv2" if kb == 3 else "skv4")
                                em.mark(("kvmm", n))
                                em.mark(("kvsp", n))

                                if n + 2 < NRING:
                                    xt_prefetch(n + 2)
                                if a < 6:
                                    emit_cc(a)
                                kvK, kvV = Ksb, Vsb
                            else:
                                kvK = xt0 if a % 2 == 0 else Ksb
                                kvV = xt1 if a % 2 == 0 else Vsb
                                if a == 0:
                                    for ap_ in (0, 1):
                                        if ap_ == 0:
                                            pass
                                        else:
                                            em.waitck("sync",
                                                      ("avmm", NRING - 1, 8), "mm")
                                            for _s in ("skv", "skv2", "skv3",
                                                        "skv4"):
                                                em.waitck("sync",
                                                          ("kvsp", NRING - 1),
                                                          _s)
                                        em.waitck("sync", ("ccd", ap_), "cc")
                                        bK = xt0 if ap_ % 2 == 0 else Ksb
                                        bV = xt1 if ap_ % 2 == 0 else Vsb
                                        kld_load(ap_, bK, bV)
                                        em.mark(("kld", ap_))
                                if a == 0:
                                    emit_cc(6)
                                    emit_cc(7)
                                for _s in ("klg", "klg2"):
                                    em.waitck("tensor", ("kldK", a), _s)
                                em.mark(("kvmm", n))

                            for kb in range(NCC):
                                if batchA:
                                    if kb < 2:
                                        em.waitck("tensor", ("kev", n, 7), "sc")
                                    else:
                                        em.waitck("tensor", ("sexp", n, kb - 2), "sc")
                                else:
                                    if kb >= 2:
                                        em.waitck("tensor", ("sexp", n, kb - 2), "sc")
                                for qs in range(nqs):
                                    w = min(512, Nq - qs * 512)
                                    for cc in range(NCC):
                                        em.op("tensor", lambda e, kb=kb, qs=qs, w=w, cc=cc, b=b, qoff=qoff, kvK=kvK:
                                              e.matmul(pK[kb % 2][:, qs * 512:qs * 512 + w],
                                                       kvK[:, cc, kb * 128:(kb + 1) * 128],
                                                       Qp[:, cc, b * 1024 + qoff + qs * 512:
                                                          b * 1024 + qoff + qs * 512 + w],
                                                       start=(cc == 0), stop=(cc == 7)),
                                              inc=("mm" if (cc == 7 and qs == nqs - 1) else None))
                                em.mark(("smm", n, kb))
                                em.waitck("scalar", ("smm", n, kb), "mm")
                                em.op("scalar", lambda e, kb=kb, Nq=Nq:
                                      e.activation(Psb[:, kb, 0:Nq], pK[kb % 2][:, 0:Nq],
                                                   AF.Exp, bias=-ESH, scale=ATT_SCALE),
                                      inc="sc")
                                em.mark(("sexp", n, kb))
                                em.waitck("vector", ("sexp", n, kb), "sc")
                                em.op("vector", lambda e, kb=kb, a=a:
                                      e.tensor_mul(Psb[:, kb, 0:128], Psb[:, kb, 0:128],
                                                   masks[:, a * 8 + kb, :]),
                                      inc="ve")
                                em.mark(("pmask", n, kb))
                            em.mark(("exps", n))

                            if not batchA:
                                for _s in ("klg3", "klg4"):
                                    em.waitck("tensor", ("kldV", a), _s)
                            for r in range(9):
                                if r < 2:
                                    em.waitck("tensor", ("pmask", n, 7), "ve")
                                else:
                                    em.waitck("tensor", ("avac", n, r - 2), "ve")
                                for qs in range(nqs):
                                    w = min(512, Nq - qs * 512)
                                    for kb in range(NCC):
                                        def mk(e, r=r, qs=qs, w=w, kb=kb, kvV=kvV):
                                            lhsT = (kvV[:, kb, r * 128:(r + 1) * 128]
                                                    if r < 8 else ones128[:, :])
                                            return e.matmul(
                                                pV[r % 2][:, qs * 512:qs * 512 + w],
                                                lhsT,
                                                Psb[:, kb, qs * 512:qs * 512 + w],
                                                start=(kb == 0), stop=(kb == 7))
                                        em.op("tensor", mk,
                                              inc=("mm" if (kb == 7 and qs == nqs - 1) else None))
                                em.mark(("avmm", n, r))
                                em.waitck("vector", ("avmm", n, r), "mm")
                                dst = (Op[:, r, qoff:1024] if r < 8
                                       else lsb[:, qoff:1024])
                                if a == 0:
                                    em.op("vector", lambda e, dst=dst, r=r, Nq=Nq:
                                          e.tensor_copy(dst, pV[r % 2][:, 0:Nq]),
                                          inc="ve")
                                else:
                                    em.op("vector", lambda e, dst=dst, r=r, Nq=Nq:
                                          e.tensor_add(dst, dst, pV[r % 2][:, 0:Nq]),
                                          inc="ve")
                                em.mark(("avac", n, r))
                            em.mark(("avacc", n))

                            if not batchA and a + 2 < NRING:
                                em.waitck("sync", ("ccd", a + 2), "cc")
                                em.waitck("sync", ("avmm", n, 8), "mm")
                                bK = xt0 if a % 2 == 0 else Ksb
                                bV = xt1 if a % 2 == 0 else Vsb
                                kld_load(a + 2, bK, bV)
                                em.mark(("kld", a + 2))

                            if a == NRING - 1 and b == 0:
                                em.wait("vector", "ve", em.cnt["ve"])
                                em.op("vector", lambda e: e.reciprocal(lsb[:, :],
                                                                       lsb[:, :]),
                                      inc="ve")
                                for cc in range(NCC):
                                    em.op("vector", lambda e, cc=cc:
                                          e.tensor_mul(on0m[:, cc, :],
                                                       Op[:, cc, :], lsb[:, :]),
                                          inc="ve")
                                em.mark(("onrm", 0))

                    run_block(sched1)
                    div_handles["Op"] = Op
                    div_handles["lsb"] = lsb

            x1t = ctx.enter_context(nc.sbuf_tensor("x1t", [128, NCC, TOK], BF16))
            yacc = yacc_m

            scopeB = contextlib.ExitStack()
            with scopeB:
                _a = scopeB.enter_context
                pj0 = _a(nc.psum_tensor("pj0", [128, C], F32))
                pj1 = _a(nc.psum_tensor("pj1", [128, C], F32))
                pj2 = _a(nc.psum_tensor("pj2", [128, C], F32))
                ptx0 = _a(nc.psum_tensor("ptx0", [128, 128], BF16))
                ptx1 = _a(nc.psum_tensor("ptx1", [128, 128], BF16))
                xst = [xst0, xst1]
                x1c = [x1c0, x1c1]
                pj = [pj0, pj1, pj2]
                ptx = [ptx0, ptx1]

                def sched2():
                    em.waitck("gpsimd", ("kvmm", NRING - 1), "mm")
                    em.waitck("gpsimd", ("kvsp", NRING - 1), "sc")
                    em.waitck("gpsimd", ("kvsp", NRING - 1), "ve")
                    em.dma("gpsimd", bpjrow_b[:, :],
                           bass.AP(bproj, 0, [[1, 1], [1, C]]), "swh")
                    em.mark("bpjb")
                    em.dma("gpsimd", wpm[:, 0:4, :],
                           bass.AP(wproj, 0, [[C, 128], [128 * C, 4], [1, C]]),
                           "swh2")
                    em.dma("gpsimd", wpm[:, 4:8, :],
                           bass.AP(wproj, 4 * 128 * C,
                                   [[C, 128], [128 * C, 4], [1, C]]),
                           "swh3")
                    em.mark("wproj")
                    for t0 in range(2):
                        em.dma("gpsimd", xst[t0][:, :],
                               xown[t0 * 128:(t0 + 1) * 128, :], "sa%d" % t0)
                        em.mark(("xst", t0))
                    em.waitck("scalar", ("avmm", 15, 8), "mm")
                    em.dma("scalar", identb2[:, :], ident_c.ap(), "scn")
                    em.mark("bpjld")

                    def emit_tr(t):
                        em.waitck("tensor", ("x1c", t), "ve")
                        em.waitck("tensor", "bpjld", "scn")
                        for cc in range(NCC):
                            if cc >= 2:
                                em.waitck("tensor", ("x1tev", t, cc - 2), "sc")
                            else:
                                em.waitck("tensor", ("x1tev", t - 1, 6 + cc), "sc")
                            em.op("tensor", lambda e, cc=cc, t=t:
                                  e.transpose(ptx[cc % 2][:, :],
                                              x1c[t % 2][:, cc * 128:(cc + 1) * 128],
                                              identb2[:, :]),
                                  inc="mm")
                            em.mark(("x1tmm", t, cc))
                            em.waitck("scalar", ("x1tmm", t, cc), "mm")
                            em.op("scalar", lambda e, t=t, cc=cc:
                                  e.activation(x1t[:, cc, t * 128:(t + 1) * 128],
                                               ptx[cc % 2][:, :], AF.Copy),
                                  inc="sc")
                            em.mark(("x1tev", t, cc))

                    for t in range(16):
                        par = t % 2
                        pq = t % 3
                        bb, tt = t // 8, t % 8
                        em.waitck("tensor", ("onrm", bb), "ve")
                        em.waitck("tensor", "wproj", "swh2")
                        em.waitck("tensor", "wproj", "swh3")
                        em.waitck("tensor", ("pjfree", t - 3), "ve")
                        em.waitck("tensor", "bpjb", "swh")
                        for h in range(2):
                            for cc in range(NCC):
                                onsrc = on0m if bb == 0 else on1m
                                em.op("tensor", lambda e, h=h, cc=cc, tt=tt, pq=pq, onsrc=onsrc:
                                      e.matmul(pj[pq][:, h * 512:(h + 1) * 512],
                                               onsrc[:, cc, tt * 128:(tt + 1) * 128],
                                               wpm[:, cc, h * 512:(h + 1) * 512],
                                               start=(cc == 0), stop=False),
                                      inc=None)
                            em.op("tensor", lambda e, h=h, pq=pq:
                                  e.matmul(pj[pq][:, h * 512:(h + 1) * 512],
                                           onesr[:, :],
                                           bpjrow_b[:, h * 512:(h + 1) * 512],
                                           start=False, stop=True),
                                  inc=("mm" if h == 1 else None))
                        em.mark(("pjmm", t))

                        if t >= 2:
                            emit_tr(t - 2)

                        em.waitck("vector", ("pjmm", t), "mm")
                        em.waitck("vector", ("xst", t), "sa%d" % par)
                        em.op("vector", lambda e, t=t, par=par, pq=pq:
                              e.scalar_tensor_tensor(yacc[:, t, :], pj[pq][:, :],
                                                     1.0, xst[par][:, :],
                                                     ALU.mult, ALU.add,
                                                     accum_out=lnt[:, t, 0:1]),
                              inc="ve")
                        em.mark(("rtdone", t))
                        em.mark(("pjfree", t))
                        em.op("vector", lambda e, t=t:
                              e.scalar_tensor_tensor(sqb0[:, :], yacc[:, t, :],
                                                     1.0, yacc[:, t, :],
                                                     ALU.mult, ALU.mult,
                                                     accum_out=lnt[:, t, 1:2]),
                              inc="ve")
                        em.op("vector", lambda e, t=t:
                              e.tensor_scalar_mul(lnt[:, t, 2:3], lnt[:, t, 0:1],
                                                  -1.0 / C),
                              inc="ve")
                        em.op("vector", lambda e, t=t:
                              e.tensor_scalar_mul(lnt[:, t, 3:4], lnt[:, t, 1:2],
                                                  1.0 / C),
                              inc="ve")
                        em.op("vector", lambda e, t=t:
                              e.tensor_mul(lnt[:, t, 4:5], lnt[:, t, 2:3],
                                           lnt[:, t, 2:3]),
                              inc="ve")
                        em.op("vector", lambda e, t=t:
                              e.tensor_sub(lnt[:, t, 5:6], lnt[:, t, 3:4],
                                           lnt[:, t, 4:5]),
                              inc="ve")
                        em.mark(("var", t))
                        em.waitck("scalar", ("var", t), "ve")
                        em.op("scalar", lambda e, t=t:
                              e.activation(lnt[:, t, 6:7], lnt[:, t, 5:6], AF.Ln,
                                           bias=EPS),
                              inc="sc")
                        em.op("scalar", lambda e, t=t:
                              e.activation(lnt[:, t, 7:8], lnt[:, t, 6:7], AF.Exp,
                                           scale=-0.5),
                              inc="sc")
                        em.mark(("rstd", t))

                        if t + 2 < 16:
                            em.waitck("scalar", ("rtdone", t), "ve")
                            em.dma("scalar", xst[par][:, :],
                                   xown[(t + 2) * 128:(t + 3) * 128, :],
                                   "sa%d" % par)
                            em.mark(("xst", t + 2))
                        em.waitck("vector", ("rstd", t), "sc")
                        em.op("vector", lambda e, t=t:
                              e.tensor_mul(lnt[:, t, 6:7], lnt[:, t, 2:3],
                                           lnt[:, t, 7:8]),
                              inc="ve")
                        em.op("vector", lambda e, t=t, par=par:
                              e.tensor_scalar(yacc[:, t, :], yacc[:, t, :],
                                              lnt[:, t, 7:8], lnt[:, t, 6:7],
                                              ALU.mult, ALU.add),
                              inc="ve")
                        em.mark(("x1", t))
                        em.waitck("vector", ("x1tmm", t - 2, NCC - 1), "mm")
                        em.op("vector", lambda e, t=t, par=par:
                              e.tensor_copy(x1c[par][:, :], yacc[:, t, :]),
                              inc="ve")
                        em.mark(("x1c", t))
                        if t == 1:
                            Op_h = div_handles["Op"]
                            lsb_h = div_handles["lsb"]
                            em.op("vector", lambda e:
                                  e.reciprocal(lsb_h[:, :], lsb_h[:, :]),
                                  inc="ve")
                            for cc in range(NCC):
                                em.op("vector", lambda e, cc=cc:
                                      e.tensor_mul(on1m[:, cc, :],
                                                   Op_h[:, cc, :], lsb_h[:, :]),
                                      inc="ve")
                            em.mark(("onrm", 1))
                    emit_tr(14)
                    emit_tr(15)

                run_block(sched2)

            scopeC = contextlib.ExitStack()
            with scopeC:
                _a = scopeC.enter_context
                ph0 = _a(nc.psum_tensor("ph0", [128, 512], F32))
                ph1 = _a(nc.psum_tensor("ph1", [128, 512], F32))
                py0 = _a(nc.psum_tensor("py0", [128, C], F32))
                py1 = _a(nc.psum_tensor("py1", [128, C], F32))
                pmt = _a(nc.psum_tensor("pmt", [8, 128], BF16))
                ph = [ph0, ph1]
                py = [py0, py1]
                w1sb = [w1a, w1b]
                w2sb = [w2a, w2b]
                sqc = [sqc0, sqc1]

                def w1ap(e_, p):
                    return bass.AP(w1, e_ * C * C + p * 4 * 128 * C,
                                   [[C, 128], [128 * C, 4], [1, C]])

                def w2ap(e_, p):
                    return bass.AP(w2, e_ * C * C + p * 4 * 128 * C,
                                   [[C, 128], [128 * C, 4], [1, C]])

                def sched3():
                    em.waitck("sync", ("onrm", 1), "ve")
                    em.dma("sync", identb[:, :], ident_c.ap(), "scn")
                    em.mark("identb_ld")
                    em.dma("sync", b1_sb[:, :, :],
                           bass.AP(b1, 0, [[1, 128], [C, 8], [128, 8]]), "scn",
                           slow=True)
                    em.mark("ffn_small")
                    em.waitck("gpsimd", ("pjmm", 15), "mm")
                    em.dma("gpsimd", wg_sb[:, :, :],
                           bass.AP(wg, 0, [[8, 128], [128 * 8, NCC], [1, 8]]),
                           "swv")
                    em.dma("gpsimd", bgrow_b[:, :],
                           bass.AP(bg, 0, [[1, 1], [1, 8]]), "swh")
                    em.waitck("gpsimd", ("var", 15), "ve")
                    em.dma("gpsimd", b2r_sb[:, :],
                           bass.AP(b2, 0, [[C, 8], [1, C]]), "swh")
                    em.mark("gate_w")
                    for e_ in range(2):
                        em.dma("gpsimd", w1sb[e_][:, 0:4, :], w1ap(e_, 0),
                               "sw1%d" % e_)
                        em.dma("gpsimd", w1sb[e_][:, 4:8, :], w1ap(e_, 1),
                               "sw1%d" % e_)
                        em.mark(("w1", e_))
                        em.dma("gpsimd", w2sb[e_][:, 0:4, :], w2ap(e_, 0),
                               "sw2%d" % e_)
                        em.dma("gpsimd", w2sb[e_][:, 4:8, :], w2ap(e_, 1),
                               "sw2%d" % e_)
                        em.mark(("w2", e_))

                    em.waitck("tensor", "gate_w", "swv")
                    em.waitck("tensor", "gate_w", "swh")
                    em.waitck("tensor", "identb_ld", "scn")
                    for t in range(16):
                        par = t % 2
                        em.waitck("tensor", ("gev", t - 2), "sc")
                        for cc in range(NCC):
                            em.op("tensor", lambda e, t=t, cc=cc, par=par:
                                  e.matmul(py[par][:, 0:8],
                                           x1t[:, cc, t * 128:(t + 1) * 128],
                                           wg_sb[:, cc, :],
                                           start=(cc == 0), stop=False),
                                  inc=None)
                        em.op("tensor", lambda e, par=par:
                              e.matmul(py[par][:, 0:8], onesr[:, :], bgrow_b[:, :],
                                       start=False, stop=True),
                              inc="mm")
                        em.mark(("gmm", t))
                        em.waitck("scalar", ("gmm", t), "mm")
                        em.op("scalar", lambda e, t=t, par=par:
                              e.activation(lg[:, t, :], py[par][:, 0:8], AF.Copy),
                              inc="sc")
                        em.mark(("gev", t))
                        em.waitck("vector", ("gev", t), "sc")
                        em.op("vector", lambda e, t=t:
                              e.reduce_max(mxc[:, :], lg[:, t, :], axis=AX.X),
                              inc="ve")
                        em.op("vector", lambda e, t=t:
                              e.tensor_scalar(mstore[:, t, :], lg[:, t, :], mxc[:, :],
                                              None, ALU.is_equal),
                              inc="ve")
                        em.waitck("vector", ("mtr", t - 1), "mm")
                        em.op("vector", lambda e, t=t:
                              e.tensor_copy(mtb[:, :], mstore[:, t, :]),
                              inc="ve")
                        em.mark(("mtb", t))
                        em.waitck("tensor", ("mtb", t), "ve")
                        em.waitck("tensor", "identb_ld", "scn")
                        em.waitck("tensor", ("mtr_ev", t - 1), "sc")
                        em.op("tensor", lambda e: e.transpose(pmt[:, :], mtb[:, :],
                                                              identb[:, :]),
                              inc="mm")
                        em.mark(("mtr", t))
                        em.waitck("scalar", ("mtr", t), "mm")
                        em.op("scalar", lambda e, t=t:
                              e.activation(maskT[:, t, :], pmt[:, :], AF.Copy),
                              inc="sc")
                        em.mark(("mtr_ev", t))
                    em.mark("maskr")

                    def emit_b2pass():
                        for t in range(16):
                            par = t % 2
                            if t < 2:
                                em.waitck("tensor", ("gev", 14 + t), "sc")
                            else:
                                em.waitck("tensor", ("b2add", t - 2), "ve")
                            em.waitck("tensor", ("mtr_ev", t), "sc")
                            for h in range(2):
                                em.op("tensor", lambda e, t=t, h=h, par=par:
                                      e.matmul(py[par][:, h * 512:(h + 1) * 512],
                                               maskT[:, t, :],
                                               b2r_sb[:, h * 512:(h + 1) * 512],
                                               start=True, stop=True),
                                      inc=("mm" if h == 1 else None))
                            em.mark(("b2mm", t))
                            em.waitck("vector", ("b2mm", t), "mm")
                            em.op("vector", lambda e, t=t, par=par:
                                  e.tensor_add(yacc[:, t, :], yacc[:, t, :],
                                               py[par][:, :]),
                                  inc="ve")
                            em.mark(("b2add", t))

                    def ln2_stage1(t):
                        em.op("vector", lambda e, t=t:
                              e.scalar_tensor_tensor(sqc[t % 2][:, :], yacc[:, t, :],
                                                     1.0, yacc[:, t, :],
                                                     ALU.mult, ALU.mult,
                                                     accum_out=lnt2[:, t, 1:2]),
                              inc="ve")
                        em.op("vector", lambda e, t=t:
                              e.tensor_scalar_mul(lnt2[:, t, 2:3], lnt2[:, t, 0:1],
                                                  -1.0 / C),
                              inc="ve")
                        em.op("vector", lambda e, t=t:
                              e.tensor_scalar_mul(lnt2[:, t, 3:4], lnt2[:, t, 1:2],
                                                  1.0 / C),
                              inc="ve")
                        em.op("vector", lambda e, t=t:
                              e.tensor_mul(lnt2[:, t, 4:5], lnt2[:, t, 2:3],
                                           lnt2[:, t, 2:3]),
                              inc="ve")
                        em.op("vector", lambda e, t=t:
                              e.tensor_sub(lnt2[:, t, 5:6], lnt2[:, t, 3:4],
                                           lnt2[:, t, 4:5]),
                              inc="ve")
                        em.mark(("var2", t))
                        em.waitck("scalar", ("var2", t), "ve")
                        em.op("scalar", lambda e, t=t:
                              e.activation(lnt2[:, t, 6:7], lnt2[:, t, 5:6], AF.Ln,
                                           bias=EPS),
                              inc="sc")
                        em.op("scalar", lambda e, t=t:
                              e.activation(lnt2[:, t, 7:8], lnt2[:, t, 6:7], AF.Exp,
                                           scale=-0.5),
                              inc="sc")
                        em.mark(("rstd2", t))

                    def ln2_finish(t):
                        em.waitck("vector", ("rstd2", t), "sc")
                        em.op("vector", lambda e, t=t:
                              e.tensor_mul(lnt2[:, t, 6:7], lnt2[:, t, 2:3],
                                           lnt2[:, t, 7:8]),
                              inc="ve")
                        em.op("vector", lambda e, t=t:
                              e.tensor_scalar(yacc[:, t, :], yacc[:, t, :],
                                              lnt2[:, t, 7:8], lnt2[:, t, 6:7],
                                              ALU.mult, ALU.add),
                              inc="ve")
                        em.mark(("ofin", t))
                        em.waitck("scalar", ("ofin", t), "ve")
                        em.dma("scalar", out[t * 128:(t + 1) * 128, :],
                               yacc[:, t, :], "so%d" % (t % 8))
                        em.mark(("ostd", t))

                    em.waitck("scalar", "ffn_small", "scn")
                    for e_ in range(8):
                        em.waitck("tensor", ("w1", e_), "sw1%d" % (e_ % 2))
                        em.waitck("tensor", ("w1", e_), "sw1%db" % (e_ % 2))
                        for th in range(2):
                            for ic in range(NCC):
                                for tq in range(2):
                                    idx = ic * 2 + tq
                                    reg = idx % 2
                                    if idx < 2:
                                        if th == 1:
                                            em.waitck("tensor",
                                                      ("hev", e_ * 2, 14 + idx), "sc")
                                        else:
                                            em.waitck("tensor",
                                                      ("hev", (e_ - 1) * 2 + 1,
                                                       14 + idx), "sc")
                                    else:
                                        em.waitck("tensor",
                                                  ("hev", e_ * 2 + th, idx - 2), "sc")
                                    for cc in range(NCC):
                                        em.op("tensor",
                                              lambda e, ic=ic, tq=tq, cc=cc, th=th, reg=reg, e_=e_:
                                              e.matmul(ph[reg][:, :],
                                                       w1sb[e_ % 2][:, cc, ic * 128:(ic + 1) * 128],
                                                       x1t[:, cc, th * 1024 + tq * 512:
                                                           th * 1024 + (tq + 1) * 512],
                                                       start=(cc == 0), stop=(cc == 7)),
                                              inc=("mm" if cc == 7 else None))
                                    em.mark(("hmm", e_ * 2 + th, idx))
                                    em.waitck("scalar", ("hmm", e_ * 2 + th, idx), "mm")
                                    em.op("scalar", lambda e, ic=ic, tq=tq, reg=reg, e_=e_:
                                          e.activation(hsb[:, ic, tq * 512:(tq + 1) * 512],
                                                       ph[reg][:, :], AF.Gelu,
                                                       bias=b1_sb[:, e_, ic:ic + 1]),
                                          inc="sc")
                                    em.mark(("hev", e_ * 2 + th, idx))
                            if th == 1:
                                em.mark(("hdone", e_))
                                if e_ + 2 < 8:
                                    em.waitck("gpsimd", ("hdone", e_), "mm")
                                    em.dma("gpsimd", w1sb[e_ % 2][:, 0:4, :],
                                           w1ap(e_ + 2, 0), "sw1%d" % (e_ % 2))
                                    em.dma("gpsimd", w1sb[e_ % 2][:, 4:8, :],
                                           w1ap(e_ + 2, 1), "sw1%d" % (e_ % 2))
                                    em.mark(("w1", e_ + 2))

                            if e_ == 0 and th == 0:
                                emit_b2pass()

                            em.waitck("tensor", ("hev", e_ * 2 + th, 15), "sc")
                            em.waitck("tensor", ("w2", e_), "sw2%d" % (e_ % 2))
                            em.waitck("tensor", ("w2", e_), "sw2%db" % (e_ % 2))
                            for tt in range(8):
                                t = th * 8 + tt
                                par = tt % 2
                                if tt < 2:
                                    if e_ == 0 and th == 0:
                                        em.waitck("tensor", ("b2add", 14 + tt), "ve")
                                    else:
                                        em.waitck("tensor",
                                                  ("yadd", e_ * 2 + th - 1, 6 + tt),
                                                  "ve")
                                else:
                                    em.waitck("tensor",
                                              ("yadd", e_ * 2 + th, tt - 2), "ve")
                                for h in range(2):
                                    for ic in range(NCC):
                                        em.op("tensor",
                                              lambda e, tt=tt, h=h, ic=ic, par=par, e_=e_:
                                              e.matmul(py[par][:, h * 512:(h + 1) * 512],
                                                       hsb[:, ic, tt * 128:(tt + 1) * 128],
                                                       w2sb[e_ % 2][:, ic, h * 512:(h + 1) * 512],
                                                       start=(ic == 0), stop=(ic == 7)),
                                              inc=("mm" if (ic == 7 and h == 1) else None))
                                em.mark(("ymm", e_ * 2 + th, tt))
                                em.waitck("vector", ("ymm", e_ * 2 + th, tt), "mm")
                                if e_ == 7:
                                    em.op("vector", lambda e, t=t, par=par, e_=e_:
                                          e.scalar_tensor_tensor(
                                              yacc[:, t, :], py[par][:, :],
                                              mstore[:, t, e_:e_ + 1], yacc[:, t, :],
                                              ALU.mult, ALU.add,
                                              accum_out=lnt2[:, t, 0:1]),
                                          inc="ve")
                                else:
                                    em.op("vector", lambda e, t=t, par=par, e_=e_:
                                          e.scalar_tensor_tensor(
                                              yacc[:, t, :], py[par][:, :],
                                              mstore[:, t, e_:e_ + 1], yacc[:, t, :],
                                              ALU.mult, ALU.add),
                                          inc="ve")
                                em.mark(("yadd", e_ * 2 + th, tt))
                                if e_ == 7:
                                    ln2_stage1(t)
                                    if t >= 1:
                                        ln2_finish(t - 1)
                            if th == 1:
                                em.mark(("ydone", e_))
                                if e_ + 2 < 8:
                                    em.waitck("gpsimd", ("ydone", e_), "mm")
                                    em.dma("gpsimd", w2sb[e_ % 2][:, 0:4, :],
                                           w2ap(e_ + 2, 0), "sw2%d" % (e_ % 2))
                                    em.dma("gpsimd", w2sb[e_ % 2][:, 4:8, :],
                                           w2ap(e_ + 2, 1), "sw2%d" % (e_ % 2))
                                    em.mark(("w2", e_ + 2))

                    ln2_finish(15)
                    for _k in range(8):
                        em.wait("scalar", "so%d" % _k, 16 * 2)

                run_block(sched3)

    return nc


_NC_CACHE = {}


def kernel(L0_attn_x, L0_attn_w_attn, L0_attn_b_attn, L0_attn_w_proj,
           L0_attn_b_proj, L0_ffn_Wg, L0_ffn_bg, L0_ffn_W1, L0_ffn_b1,
           L0_ffn_W2, L0_ffn_b2):
    x = np.ascontiguousarray(np.asarray(L0_attn_x, dtype=np.float32))
    x2 = x.reshape(B * T, C)
    shared = {
        "wattn": np.ascontiguousarray(np.asarray(L0_attn_w_attn, np.float32)),
        "battn": np.ascontiguousarray(np.asarray(L0_attn_b_attn, np.float32)),
        "wproj": np.ascontiguousarray(np.asarray(L0_attn_w_proj, np.float32)),
        "bproj": np.ascontiguousarray(np.asarray(L0_attn_b_proj, np.float32)),
        "wg": np.ascontiguousarray(np.asarray(L0_ffn_Wg, np.float32)),
        "bg": np.ascontiguousarray(np.asarray(L0_ffn_bg, np.float32)),
        "w1": np.ascontiguousarray(np.asarray(L0_ffn_W1, np.float32)),
        "b1": np.ascontiguousarray(np.asarray(L0_ffn_b1, np.float32)),
        "w2": np.ascontiguousarray(np.asarray(L0_ffn_W2, np.float32)),
        "b2": np.ascontiguousarray(np.asarray(L0_ffn_b2, np.float32)),
    }

    xTp = [np.ascontiguousarray(x2[0:T].T),
           np.ascontiguousarray(x2[T:2 * T].T)]
    tri = np.triu(np.ones((128, 128), np.float32))
    in_maps = []
    for i in range(NCORE):
        blocks = _blocks_for_core(i)
        bA = i % 2
        view_batches = [bA, 1 - bA]
        rows = []
        for b in view_batches:
            for m in range(8):
                blk = blocks[m]
                rows.append(np.arange(blk * 128, (blk + 1) * 128) + b * T)
        rows = np.concatenate(rows)
        xo = np.ascontiguousarray(x2[rows])

        qm = np.zeros((64, 128, 128), np.float32)
        for k in range(64):
            a, r = k // 8, k % 8
            sig = i if a % 2 == 0 else 7 - i
            if sig > r:
                qm[k] = 1.0
            elif sig == r:
                qm[k] = tri
        qm = qm.reshape(64 * 128, 128).astype(BF)

        m = dict(shared)
        m["xT"] = xTp[bA]
        m["xown"] = xo
        m["xownT"] = np.ascontiguousarray(xo.T)
        m["qmask"] = np.ascontiguousarray(qm)
        in_maps.append(m)

    if "full" not in _NC_CACHE:
        _NC_CACHE["full"] = build()
    nc = _NC_CACHE["full"]

    res = run_bass_kernel_spmd(nc, in_maps, core_ids=list(range(NCORE)))

    outf = np.empty((B, T, C), np.float32)
    for i in range(NCORE):
        o = res.results[i]["out"]
        blocks = _blocks_for_core(i)
        bA = i % 2
        view_batches = [bA, 1 - bA]
        for vb, b in enumerate(view_batches):
            for m in range(8):
                blk = blocks[m]
                outf[b, blk * 128:(blk + 1) * 128, :] = \
                    o[(vb * 8 + m) * 128:(vb * 8 + m + 1) * 128, :]
    return outf


# version 18
# speedup vs baseline: 1.1220x; 1.0722x over previous
import contextlib

import numpy as np
import ml_dtypes
import concourse.bass as bass
import concourse.mybir as mybir
from concourse.bass_utils import run_bass_kernel_spmd

F32 = mybir.dt.float32
BF16 = mybir.dt.bfloat16
AF = mybir.ActivationFunctionType
ALU = mybir.AluOpType
AX = mybir.AxisListType
BF = ml_dtypes.bfloat16

B, T, C = 2, 8192, 1024
NCORE = 8
TOK = 2048
NCC = 8
NRING = 8
RK = 1024
ATT_SCALE = 1.0 / 32.0
ESH = 8.0
EPS = 1e-5

SEMS = ("mm", "sc", "ve",
        "sxc0", "sxc1", "sxc2", "sxc3",
        "swv",
        "sxo",
        "sxt",
        "scn",
        "st00", "st01", "st02", "st03",
        "st10", "st11", "st12", "st13",
        "skv", "skv2", "skv3", "skv4",
        "swh",
        "swh2", "swh3",
        "cc",
        "klg", "klg2", "klg3", "klg4",
        "ssp", "ssp2",
        "son", "son2", "son3", "son4",
        "sa0", "sa1",
        "sw10", "sw11", "sw20", "sw21",
        "sw10b", "sw11b", "sw20b", "sw21b",
        "so0", "so1", "so2", "so3",
        "so4", "so5", "so6", "so7")


def _blocks_for_core(i):
    return [8 * m + (i if m % 2 == 0 else 7 - i) for m in range(8)]


def _has_neg(key):
    if isinstance(key, tuple):
        return any(isinstance(k, int) and k < 0 for k in key)
    return False


class Emitter:

    def __init__(self, nc, sems):
        self.nc = nc
        self.sems = sems

    def begin(self, lab, eng):
        self.lab = lab
        self.eng = eng
        self.cnt = {s: 0 for s in SEMS}
        self.ck = {}
        self.lastwait = {}

    def wait(self, engine, sem, val):
        if val is None or val <= 0:
            return
        if self.lab == engine:
            if self.lastwait.get(sem, -1) >= val:
                return
            self.lastwait[sem] = val
            self.eng.wait_ge(self.sems[sem], val)

    def waitck(self, engine, key, sem):
        if key not in self.ck:
            if _has_neg(key):
                return
            raise KeyError(f"waitck: missing checkpoint {key!r}")
        self.wait(engine, sem, self.ck[key][sem])

    def waitall(self, engine, sem):
        self.wait(engine, sem, self.cnt[sem])

    def op(self, engine, build, inc=None, incn=1):
        if engine == "vector":
            self.wait(engine, "ve", self.cnt["ve"])
        elif engine == "scalar" and inc == "sc":
            self.wait(engine, "sc", self.cnt["sc"])
        if self.lab == engine:
            ins = build(self.eng)
            if inc is not None:
                ins.then_inc(self.sems[inc], incn)
        if inc is not None:
            self.cnt[inc] += incn

    def dma(self, engine, out, in_, inc, transpose=False, slow=False):
        self.wait(engine, inc, self.cnt[inc])
        if slow:
            self.op(engine,
                    lambda e: e.dma_start(out, in_,
                                          allow_slow_non_contiguous=True),
                    inc=inc, incn=16)
        else:
            self.op(engine, lambda e: e.dma_start(out, in_, transpose=transpose),
                    inc=inc, incn=16)

    def mark(self, key):
        self.ck[key] = dict(self.cnt)

    def at(self, key, sem):
        return self.ck[key][sem] if key in self.ck else 0


def build():
    nc = bass.Bass("TRN2", target_bir_lowering=False)

    xT = nc.dram_tensor("xT", [C, T], F32, kind="ExternalInput")
    xown = nc.dram_tensor("xown", [TOK, C], F32, kind="ExternalInput")
    xownT = nc.dram_tensor("xownT", [C, TOK], F32, kind="ExternalInput")
    wattn = nc.dram_tensor("wattn", [C, 3 * C], F32, kind="ExternalInput")
    battn = nc.dram_tensor("battn", [3 * C], F32, kind="ExternalInput")
    wproj = nc.dram_tensor("wproj", [C, C], F32, kind="ExternalInput")
    bproj = nc.dram_tensor("bproj", [C], F32, kind="ExternalInput")
    wg = nc.dram_tensor("wg", [C, 8], F32, kind="ExternalInput")
    bg = nc.dram_tensor("bg", [8], F32, kind="ExternalInput")
    w1 = nc.dram_tensor("w1", [8, C, C], F32, kind="ExternalInput")
    b1 = nc.dram_tensor("b1", [8, C], F32, kind="ExternalInput")
    w2 = nc.dram_tensor("w2", [8, C, C], F32, kind="ExternalInput")
    b2 = nc.dram_tensor("b2", [8, C], F32, kind="ExternalInput")
    qmask = nc.dram_tensor("qmask", [64 * 128, 128], BF16, kind="ExternalInput")

    out = nc.dram_tensor("out", [TOK, C], F32, kind="ExternalOutput")

    wbp = nc.dram_tensor("wbp", [C, C], BF16)
    wb1 = nc.dram_tensor("wb1", [8, C, C], BF16)
    wb2 = nc.dram_tensor("wb2", [8, C, C], BF16)
    wbg = nc.dram_tensor("wbg", [C, 8], BF16)
    KVM = 128 * NCC * RK
    kvsend = nc.dram_tensor("kvsend", [NRING, 2, 128, NCC * RK], BF16)
    kvgath = nc.dram_tensor("kvgath", [NRING, 2, 2, 128, NCC * RK], BF16)

    consts_np = np.zeros((128, 2), np.float32)
    consts_np[:, 0] = -ESH
    consts_np[:, 1] = EPS
    consts_c = nc.inline_tensor(consts_np, "fconsts_c")
    ones_r = nc.inline_tensor(np.ones((1, 128), BF), "onesr_c")
    ones_sq = nc.inline_tensor(np.ones((128, 128), BF), "ones128_c")
    ident_c = nc.inline_tensor(np.eye(128).astype(BF), "ident_c")
    ehot_np = np.zeros((8, 8, 128), np.float32)
    for e_ in range(8):
        ehot_np[e_, e_, :] = 1.0
    ehot_c = nc.inline_tensor(ehot_np.transpose(1, 0, 2).astype(BF), "ehot_c")

    ctx = contextlib.ExitStack()
    sems = {s: ctx.enter_context(nc.semaphore(s)) for s in SEMS}
    em = Emitter(nc, sems)

    chunks = [(b, a) for b in range(B) for a in range(NRING)]

    persist_ck = {}
    persist_cnt = {s: 0 for s in SEMS}

    def run_block(schedule_fn):
        saved = {}
        with nc.Block(no_gpsimd_drain=True) as blk:
            for lab, reg in (("tensor", blk.tensor), ("scalar", blk.scalar),
                             ("vector", blk.vector), ("sync", blk.sync),
                             ("gpsimd", blk.gpsimd)):
                def f(eng, lab=lab):
                    em.begin(lab, eng)
                    em.ck.update(persist_ck)
                    em.cnt.update(persist_cnt)
                    schedule_fn()
                    saved["ck"] = dict(em.ck)
                    saved["cnt"] = dict(em.cnt)
                reg(f)
        persist_ck.update(saved["ck"])
        persist_cnt.update(saved["cnt"])

    with ctx:
        with nc.Block(no_gpsimd_drain=True) as _blk0:
            def _clr(eng):
                for _s in SEMS:
                    eng.sem_clear(sems[_s])
            _blk0.sync(_clr)
        onesr = ctx.enter_context(nc.sbuf_tensor("onesr", [1, 128], BF16))
        cst_sb = ctx.enter_context(nc.sbuf_tensor("cst_sb", [128, 2], F32))
        nc.const_aps.register(F32, -ESH, cst_sb[:, 0:1])
        nc.const_aps.register(F32, EPS, cst_sb[:, 1:2])
        div_handles = {}
        if True:
            scopeA = contextlib.ExitStack()
            with scopeA:
                Qp = scopeA.enter_context(nc.sbuf_tensor("Qp", [128, NCC, TOK], BF16))
                wk_sb = scopeA.enter_context(nc.sbuf_tensor("wk_sb", [128, NCC, C], BF16))
                wv_sb = scopeA.enter_context(nc.sbuf_tensor("wv_sb", [128, NCC, C], BF16))
                masks = scopeA.enter_context(nc.sbuf_tensor("masks", [128, 64, 128], BF16))
                ones128 = scopeA.enter_context(nc.sbuf_tensor("ones128", [128, 128], BF16))
                bq_sb = scopeA.enter_context(nc.sbuf_tensor("bq_sb", [128, NCC], F32))
                bk_sb = scopeA.enter_context(nc.sbuf_tensor("bk_sb", [128, NCC], F32))
                bvrow = scopeA.enter_context(nc.sbuf_tensor("bvrow", [1, C], F32))
                bvrow_b = scopeA.enter_context(nc.sbuf_tensor("bvrow_b", [1, C], BF16))
                bvb = scopeA.enter_context(nc.sbuf_tensor("bvb", [128, C], F32))

                scope0 = contextlib.ExitStack()
                with scope0:
                    wq_sb = scope0.enter_context(nc.sbuf_tensor("wq_sb", [128, NCC, C], BF16))
                    xto = scope0.enter_context(nc.sbuf_tensor("xto", [128, NCC, TOK], BF16))
                    qps0 = scope0.enter_context(nc.psum_tensor("qps0", [128, 512], F32))
                    qps1 = scope0.enter_context(nc.psum_tensor("qps1", [128, 512], F32))
                    bvps = scope0.enter_context(nc.psum_tensor("bvps", [128, C], F32))
                    qps = [qps0, qps1]

                    def sched0():
                        for q in range(4):
                            em.dma("gpsimd", xto[:, :, q * 512:(q + 1) * 512],
                                   bass.AP(xownT, q * 512,
                                           [[TOK, 128], [128 * TOK, NCC],
                                            [1, 512]]),
                                   "st0%d" % q)
                            em.mark(("xtp", q))
                        for name, sb, co0 in (("wq", wq_sb, 0), ("wk", wk_sb, C),
                                              ("wv", wv_sb, 2 * C)):
                            em.dma("gpsimd", sb[:, :, :],
                                   bass.AP(wattn, co0,
                                           [[3 * C, 128], [128 * 3 * C, NCC], [1, C]]),
                                   "swv")
                        em.mark("wqkv")

                        em.dma("sync", cst_sb[:, :], consts_c.ap(), "scn")
                        em.dma("sync", onesr[:, :], ones_r.ap(), "scn")
                        em.dma("sync", ones128[:, :], ones_sq.ap(), "scn")
                        em.dma("sync", masks[:, :, :],
                               bass.AP(qmask, 0, [[128, 128], [128 * 128, 64], [1, 128]]),
                               "scn")
                        em.dma("sync", bq_sb[:, :],
                               bass.AP(battn, 0, [[1, 128], [128, NCC]]), "scn",
                               slow=True)
                        em.dma("sync", bk_sb[:, :],
                               bass.AP(battn, C, [[1, 128], [128, NCC]]), "scn",
                               slow=True)
                        em.dma("sync", bvrow[:, :],
                               bass.AP(battn, 2 * C, [[1, 1], [1, C]]), "scn")
                        em.mark("consts")

                        em.waitck("scalar", "consts", "scn")
                        em.op("scalar", lambda e: e.copy(bvrow_b[:, :], bvrow[:, :]),
                              inc="sc")
                        em.mark("bvrow_b")

                        em.waitck("tensor", "wqkv", "swv")
                        em.waitck("tensor", "consts", "scn")
                        for co in range(NCC):
                            for qs in range(TOK // 512):
                                idx = co * 4 + qs
                                r = idx % 2
                                em.waitck("tensor", ("xtp", qs), "st0%d" % qs)
                                em.waitck("tensor", ("qev", idx - 2), "sc")
                                for ci in range(NCC):
                                    em.op("tensor", lambda e, co=co, qs=qs, ci=ci, r=r:
                                          e.matmul(qps[r][:, :],
                                                   wq_sb[:, ci, co * 128:(co + 1) * 128],
                                                   xto[:, ci, qs * 512:(qs + 1) * 512],
                                                   start=(ci == 0), stop=(ci == 7)),
                                          inc=("mm" if ci == 7 else None))
                                em.mark(("qmm", idx))
                                em.waitck("scalar", ("qmm", idx), "mm")
                                em.op("scalar", lambda e, co=co, qs=qs, r=r:
                                      e.activation(Qp[:, co, qs * 512:(qs + 1) * 512],
                                                   qps[r][:, :], AF.Identity,
                                                   bias=bq_sb[:, co:co + 1]),
                                      inc="sc")
                                em.mark(("qev", idx))
                        em.waitck("tensor", "bvrow_b", "sc")
                        for h in range(2):
                            em.op("tensor", lambda e, h=h:
                                  e.matmul(bvps[:, h * 512:(h + 1) * 512], onesr[:, :],
                                           bvrow_b[:, h * 512:(h + 1) * 512],
                                           start=True, stop=True),
                                  inc=("mm" if h == 1 else None))
                        em.mark("bvmm")
                        em.waitck("scalar", "bvmm", "mm")
                        em.op("scalar", lambda e: e.activation(bvb[:, :], bvps[:, :],
                                                               AF.Copy), inc="sc")
                        em.mark("bvb")

                    run_block(sched0)

                scope1 = contextlib.ExitStack()
                with scope1:
                    xt0 = scope1.enter_context(nc.sbuf_tensor("xt0", [128, NCC, RK], BF16))
                    xt1 = scope1.enter_context(nc.sbuf_tensor("xt1", [128, NCC, RK], BF16))
                    Ksb = scope1.enter_context(nc.sbuf_tensor("Ksb", [128, NCC, RK], BF16))
                    Vsb = scope1.enter_context(nc.sbuf_tensor("Vsb", [128, NCC, RK], BF16))
                    Psb = scope1.enter_context(nc.sbuf_tensor("Psb", [128, NCC, RK], BF16))
                    Op = scope1.enter_context(nc.sbuf_tensor("Op", [128, NCC, 1024], F32))
                    lsb = scope1.enter_context(nc.sbuf_tensor("lsb", [128, 1024], F32))
                    pK0 = scope1.enter_context(nc.psum_tensor("pK0", [128, RK], F32))
                    pK1 = scope1.enter_context(nc.psum_tensor("pK1", [128, RK], F32))
                    pV0 = scope1.enter_context(nc.psum_tensor("pV0", [128, RK], F32))
                    pV1 = scope1.enter_context(nc.psum_tensor("pV1", [128, RK], F32))
                    xts = [xt0, xt1]
                    pK = [pK0, pK1]
                    pV = [pV0, pV1]
                    rkv = scope1.enter_context(nc.sync.register("rkv"))
                    rtmp = scope1.enter_context(nc.sync.register("rtmp"))

                    AT = nc.alloc_sbuf_tensor_at
                    adr = lambda h: nc.lookup_mloc(h).addr
                    on0m = AT("on0m", [128, NCC, 1024], BF16, offset=adr(wv_sb))
                    on1m = AT("on1m", [128, NCC, 1024], BF16, offset=adr(masks))
                    wpm = AT("wpm", [128, NCC, C], BF16, offset=adr(wk_sb))
                    xst0 = AT("xst0", [128, C], F32, offset=adr(bq_sb))
                    xst1 = AT("xst1", [128, C], F32, offset=adr(bvrow_b))
                    bpjrow_b = AT("bpjrow_b", [1, C], BF16,
                                  offset=adr(bvrow_b) + 4096)
                    assert adr(xt1) == adr(xt0) + 16384
                    assert adr(Ksb) == adr(xt0) + 2 * 16384
                    assert adr(Vsb) == adr(xt0) + 3 * 16384
                    yacc_m = AT("yacc", [128, 16, C], F32, offset=adr(xt0))
                    sqb0 = AT("sqb0", [128, C], F32, offset=adr(Psb))
                    lnt = AT("lnt", [128, 16, 8], F32, offset=adr(Psb) + 4096)
                    identb2 = AT("identb2", [128, 128], BF16,
                                 offset=adr(Psb) + 4608)
                    x1c0 = AT("x1c0", [128, C], BF16, offset=adr(Psb) + 4864)
                    x1c1 = AT("x1c1", [128, C], BF16, offset=adr(Psb) + 6912)
                    w1a = AT("w1a", [128, NCC, C], BF16, offset=adr(wv_sb))
                    w1b = AT("w1b", [128, NCC, C], BF16, offset=adr(wk_sb))
                    w2a = AT("w2a", [128, NCC, C], BF16, offset=adr(masks))
                    w2b = AT("w2b", [128, NCC, C], BF16,
                             offset=adr(Psb) + 8960)
                    hsb = AT("hsb", [128, NCC, 1024], BF16,
                             offset=adr(Psb) + 8960 + 16384)
                    sqc0 = AT("sqc0", [128, C], F32,
                              offset=adr(Psb) + 8960 + 2 * 16384)
                    sqc1 = AT("sqc1", [128, C], F32,
                              offset=adr(Psb) + 8960 + 2 * 16384 + 4096)
                    lnt2 = AT("lnt2", [128, 16, 8], F32,
                              offset=adr(Psb) + 8960 + 2 * 16384 + 2 * 4096)
                    _sm = adr(Psb) + 8960 + 2 * 16384 + 2 * 4096 + 512
                    wg_sb = AT("wg_sb", [128, NCC, 8], BF16, offset=_sm)
                    bgrow_b = AT("bgrow_b", [1, 8], BF16, offset=_sm + 128)
                    b1_sb = AT("b1_sb", [128, 8, 8], F32, offset=_sm + 160)
                    identb = AT("identb", [128, 128], BF16, offset=_sm + 416)
                    lg = AT("lg", [128, 16, 8], F32, offset=_sm + 672)
                    mstore = AT("mstore", [128, 16, 8], F32, offset=_sm + 1184)
                    mxc = AT("mxc", [128, 1], F32, offset=_sm + 1696)
                    mtb = AT("mtb", [128, 8], BF16, offset=_sm + 1728)
                    maskT = AT("maskT", [8, 16, 128], BF16,
                               offset=adr(Psb) + 4864)
                    b2r_sb = AT("b2r_sb", [8, C], BF16, offset=adr(Psb))
                    assert _sm + 1760 <= adr(lsb) + 4096
                    nc._b2_handles = (on0m, on1m, wpm, xst0, xst1, bpjrow_b,
                                      yacc_m, sqb0, lnt, identb2, x1c0, x1c1,
                                      w1a, w1b, w2a, w2b, hsb, sqc0, sqc1,
                                      lnt2, wg_sb, bgrow_b, b1_sb, identb, lg,
                                      mstore, mxc, mtb, maskT, b2r_sb)

                    def sched1():
                        i_batchA = chunks[0][0]

                        def xt_prefetch(n):
                            b, a = chunks[n]
                            if n >= 2:
                                em.waitck("gpsimd", ("kvmm", n - 2), "mm")
                            for p in range(2):
                                em.dma("gpsimd",
                                       xts[n % 2][:, p * 4:(p + 1) * 4, :],
                                       bass.AP(xT, a * RK + p * 4 * 128 * T,
                                               [[T, 128], [128 * T, 4],
                                                [1, RK]]),
                                       "st%d%d" % (n % 2, p))
                            em.mark(("xt", n))

                        def emit_cc(a):
                            for _s in ("skv", "skv2", "skv3", "skv4"):
                                em.waitck("gpsimd", ("kvsp", a), _s)
                            em.op("gpsimd", lambda e, a=a:
                                  e.collective_compute(
                                      "AllGather", ALU.bypass,
                                      replica_groups=[[0, 1], [2, 3], [4, 5], [6, 7]],
                                      ins=[kvsend[a, :, :, :].opt()],
                                      outs=[kvgath[a, :, :, :, :].opt()]),
                                  inc="cc")
                            em.mark(("ccd", a))

                        def kld_load(ring, dstK, dstV):
                            npc = 2 if ring < 2 else 1
                            w = NCC // npc
                            for kv, dstb in ((0, dstK), (1, dstV)):
                                for p in range(npc):
                                    if npc == 2:
                                        s = (("klg", "klg2") if kv == 0
                                             else ("klg3", "klg4"))[p]
                                    else:
                                        s = (("klg", "klg2") if kv == 0
                                             else ("klg3", "klg4"))[ring % 2]
                                    em.wait("sync", s, em.cnt[s])
                                    em.op("sync",
                                          lambda e, ring=ring, kv=kv, p=p,
                                          dstb=dstb, w=w:
                                          (e.reg_alu(rkv, rtmp,
                                                     (ring * 4 + kv) * KVM
                                                     + p * w * RK, ALU.add),
                                           e.dma_start(
                                               dstb[:, p * w:(p + 1) * w, :],
                                               bass.AP(kvgath, rkv,
                                                       [[NCC * RK, 128],
                                                        [1, w * RK]]))
                                           )[1],
                                          inc=s, incn=16)

                        xt_prefetch(0)
                        xt_prefetch(1)

                        em.waitck("tensor", "consts", "scn")
                        em.waitck("vector", "consts", "scn")
                        em.waitck("scalar", "consts", "scn")

                        def g_slotoff(e):
                            pid = nc.partition_id(engines=[mybir.EngineType.SP])
                            e.reg_alu(rtmp, pid, 1, ALU.bitwise_and)
                            e.reg_alu(rtmp, rtmp, 1, ALU.bitwise_xor)
                            return e.reg_alu(rtmp, rtmp, 2 * KVM, ALU.mult)
                        em.op("sync", g_slotoff, inc=None)

                        for n, (b, a) in enumerate(chunks):
                            p2 = n % 2
                            qoff = a * 128
                            Nq = 1024 - qoff
                            nqs = (Nq + 511) // 512
                            batchA = n < NRING

                            if batchA:
                                for _l in range(2):
                                    em.waitck("tensor", ("xt", n),
                                              "st%d%d" % (p2, _l))
                                em.waitck("tensor", ("exps", n - 1), "sc")
                                for cc in range(NCC):
                                    em.waitck("tensor", ("kev", n, cc - 2), "sc")
                                    for h in range(2):
                                        for ci in range(NCC):
                                            em.op("tensor", lambda e, cc=cc, h=h, ci=ci, p2=p2:
                                                  e.matmul(pK[cc % 2][:, h * 512:(h + 1) * 512],
                                                           wk_sb[:, ci, cc * 128:(cc + 1) * 128],
                                                           xts[p2][:, ci, h * 512:(h + 1) * 512],
                                                           start=(ci == 0), stop=(ci == 7)),
                                                  inc=("mm" if (h == 1 and ci == 7) else None))
                                    em.mark(("kmm", n, cc))
                                    em.waitck("scalar", ("kmm", n, cc), "mm")
                                    if cc == 0:
                                        for _s in ("skv", "skv2", "skv3",
                                                    "skv4"):
                                            em.waitck("scalar",
                                                      ("kvsp", n - 1), _s)
                                    em.op("scalar", lambda e, cc=cc:
                                          e.activation(Ksb[:, cc, :], pK[cc % 2][:, :],
                                                       AF.Identity, bias=bk_sb[:, cc:cc + 1]),
                                          inc="sc")
                                    em.mark(("kev", n, cc))
                                    if cc in (3, 7):
                                        em.waitck("sync", ("kev", n, cc), "sc")
                                        em.dma("sync",
                                               kvsend[a, 0, :, (cc - 3) * 1024:
                                                      (cc + 1) * 1024],
                                               Ksb[:, cc - 3:cc + 1, :],
                                               "skv" if cc == 3 else "skv3")

                                em.waitck("tensor", ("avacc", n - 1), "ve")
                                em.waitck("vector", "bvb", "sc")
                                for kb in range(NCC):
                                    em.waitck("tensor", ("vev", n, kb - 2), "ve")
                                    for h in range(2):
                                        for ci in range(NCC):
                                            em.op("tensor", lambda e, kb=kb, h=h, ci=ci, p2=p2:
                                                  e.matmul(pV[kb % 2][:, h * 512:(h + 1) * 512],
                                                           xts[p2][:, ci, kb * 128:(kb + 1) * 128],
                                                           wv_sb[:, ci, h * 512:(h + 1) * 512],
                                                           start=(ci == 0), stop=(ci == 7)),
                                                  inc=("mm" if (h == 1 and ci == 7) else None))
                                    em.mark(("vmm", n, kb))
                                    em.waitck("vector", ("vmm", n, kb), "mm")
                                    if kb == 0:
                                        for _s in ("skv", "skv2", "skv3",
                                                    "skv4"):
                                            em.waitck("vector",
                                                      ("kvsp", n - 1), _s)
                                    em.op("vector", lambda e, kb=kb:
                                          e.tensor_tensor(Vsb[:, kb, :], pV[kb % 2][:, :],
                                                          bvb[:, :], ALU.add),
                                          inc="ve")
                                    em.mark(("vev", n, kb))
                                    if kb in (3, 7):
                                        em.waitck("sync", ("vev", n, kb), "ve")
                                        em.dma("sync",
                                               kvsend[a, 1, :, (kb - 3) * 1024:
                                                      (kb + 1) * 1024],
                                               Vsb[:, kb - 3:kb + 1, :],
                                               "skv2" if kb == 3 else "skv4")
                                em.mark(("kvmm", n))
                                em.mark(("kvsp", n))

                                if n + 2 < NRING:
                                    xt_prefetch(n + 2)
                                if a < 6:
                                    emit_cc(a)
                                kvK, kvV = Ksb, Vsb
                            else:
                                kvK = xt0 if a % 2 == 0 else Ksb
                                kvV = xt1 if a % 2 == 0 else Vsb
                                if a == 0:
                                    for ap_ in (0, 1):
                                        if ap_ == 0:
                                            em.waitck("gpsimd",
                                                      ("kvmm", NRING - 1), "mm")
                                        else:
                                            em.waitck("sync",
                                                      ("avmm", NRING - 1, 8), "mm")
                                            for _s in ("skv", "skv2", "skv3",
                                                        "skv4"):
                                                em.waitck("sync",
                                                          ("kvsp", NRING - 1),
                                                          _s)
                                        em.waitck("sync", ("ccd", ap_), "cc")
                                        bK = xt0 if ap_ % 2 == 0 else Ksb
                                        bV = xt1 if ap_ % 2 == 0 else Vsb
                                        kld_load(ap_, bK, bV)
                                        em.mark(("kld", ap_))
                                if a == 0:
                                    emit_cc(6)
                                    emit_cc(7)
                                for _s in ("klg", "klg2", "klg3", "klg4"):
                                    em.waitck("tensor", ("kld", a), _s)
                                em.mark(("kvmm", n))

                            for kb in range(NCC):
                                if batchA:
                                    if kb < 2:
                                        em.waitck("tensor", ("kev", n, 7), "sc")
                                    else:
                                        em.waitck("tensor", ("sexp", n, kb - 2), "sc")
                                else:
                                    if kb >= 2:
                                        em.waitck("tensor", ("sexp", n, kb - 2), "sc")
                                for qs in range(nqs):
                                    w = min(512, Nq - qs * 512)
                                    for cc in range(NCC):
                                        em.op("tensor", lambda e, kb=kb, qs=qs, w=w, cc=cc, b=b, qoff=qoff, kvK=kvK:
                                              e.matmul(pK[kb % 2][:, qs * 512:qs * 512 + w],
                                                       kvK[:, cc, kb * 128:(kb + 1) * 128],
                                                       Qp[:, cc, b * 1024 + qoff + qs * 512:
                                                          b * 1024 + qoff + qs * 512 + w],
                                                       start=(cc == 0), stop=(cc == 7)),
                                              inc=("mm" if (cc == 7 and qs == nqs - 1) else None))
                                em.mark(("smm", n, kb))
                                em.waitck("scalar", ("smm", n, kb), "mm")
                                em.op("scalar", lambda e, kb=kb, Nq=Nq:
                                      e.activation(Psb[:, kb, 0:Nq], pK[kb % 2][:, 0:Nq],
                                                   AF.Exp, bias=-ESH, scale=ATT_SCALE),
                                      inc="sc")
                                em.mark(("sexp", n, kb))
                                em.waitck("vector", ("sexp", n, kb), "sc")
                                em.op("vector", lambda e, kb=kb, a=a:
                                      e.tensor_mul(Psb[:, kb, 0:128], Psb[:, kb, 0:128],
                                                   masks[:, a * 8 + kb, :]),
                                      inc="ve")
                                em.mark(("pmask", n, kb))
                            em.mark(("exps", n))

                            for r in range(9):
                                if r < 2:
                                    em.waitck("tensor", ("pmask", n, 7), "ve")
                                else:
                                    em.waitck("tensor", ("avac", n, r - 2), "ve")
                                for qs in range(nqs):
                                    w = min(512, Nq - qs * 512)
                                    for kb in range(NCC):
                                        def mk(e, r=r, qs=qs, w=w, kb=kb, kvV=kvV):
                                            lhsT = (kvV[:, kb, r * 128:(r + 1) * 128]
                                                    if r < 8 else ones128[:, :])
                                            return e.matmul(
                                                pV[r % 2][:, qs * 512:qs * 512 + w],
                                                lhsT,
                                                Psb[:, kb, qs * 512:qs * 512 + w],
                                                start=(kb == 0), stop=(kb == 7))
                                        em.op("tensor", mk,
                                              inc=("mm" if (kb == 7 and qs == nqs - 1) else None))
                                em.mark(("avmm", n, r))
                                em.waitck("vector", ("avmm", n, r), "mm")
                                dst = (Op[:, r, qoff:1024] if r < 8
                                       else lsb[:, qoff:1024])
                                if a == 0:
                                    em.op("vector", lambda e, dst=dst, r=r, Nq=Nq:
                                          e.tensor_copy(dst, pV[r % 2][:, 0:Nq]),
                                          inc="ve")
                                else:
                                    em.op("vector", lambda e, dst=dst, r=r, Nq=Nq:
                                          e.tensor_add(dst, dst, pV[r % 2][:, 0:Nq]),
                                          inc="ve")
                                em.mark(("avac", n, r))
                            em.mark(("avacc", n))

                            if not batchA and a + 2 < NRING:
                                em.waitck("sync", ("ccd", a + 2), "cc")
                                em.waitck("sync", ("avmm", n, 8), "mm")
                                bK = xt0 if a % 2 == 0 else Ksb
                                bV = xt1 if a % 2 == 0 else Vsb
                                kld_load(a + 2, bK, bV)
                                em.mark(("kld", a + 2))

                            if a == NRING - 1 and b == 0:
                                em.wait("vector", "ve", em.cnt["ve"])
                                em.op("vector", lambda e: e.reciprocal(lsb[:, :],
                                                                       lsb[:, :]),
                                      inc="ve")
                                for cc in range(NCC):
                                    em.op("vector", lambda e, cc=cc:
                                          e.tensor_mul(on0m[:, cc, :],
                                                       Op[:, cc, :], lsb[:, :]),
                                          inc="ve")
                                em.mark(("onrm", 0))

                    run_block(sched1)
                    div_handles["Op"] = Op
                    div_handles["lsb"] = lsb

            x1t = ctx.enter_context(nc.sbuf_tensor("x1t", [128, NCC, TOK], BF16))
            yacc = yacc_m

            scopeB = contextlib.ExitStack()
            with scopeB:
                _a = scopeB.enter_context
                pj0 = _a(nc.psum_tensor("pj0", [128, C], F32))
                pj1 = _a(nc.psum_tensor("pj1", [128, C], F32))
                pj2 = _a(nc.psum_tensor("pj2", [128, C], F32))
                ptx0 = _a(nc.psum_tensor("ptx0", [128, 128], BF16))
                ptx1 = _a(nc.psum_tensor("ptx1", [128, 128], BF16))
                xst = [xst0, xst1]
                x1c = [x1c0, x1c1]
                pj = [pj0, pj1, pj2]
                ptx = [ptx0, ptx1]

                def sched2():
                    em.waitck("gpsimd", ("kvmm", NRING - 1), "mm")
                    em.waitck("gpsimd", ("kvsp", NRING - 1), "sc")
                    em.waitck("gpsimd", ("kvsp", NRING - 1), "ve")
                    em.dma("gpsimd", bpjrow_b[:, :],
                           bass.AP(bproj, 0, [[1, 1], [1, C]]), "swh")
                    em.mark("bpjb")
                    em.dma("gpsimd", wpm[:, 0:4, :],
                           bass.AP(wproj, 0, [[C, 128], [128 * C, 4], [1, C]]),
                           "swh2")
                    em.dma("gpsimd", wpm[:, 4:8, :],
                           bass.AP(wproj, 4 * 128 * C,
                                   [[C, 128], [128 * C, 4], [1, C]]),
                           "swh3")
                    em.mark("wproj")
                    for t0 in range(2):
                        em.dma("gpsimd", xst[t0][:, :],
                               xown[t0 * 128:(t0 + 1) * 128, :], "sa%d" % t0)
                        em.mark(("xst", t0))
                    em.waitck("scalar", ("avmm", 15, 8), "mm")
                    em.dma("scalar", identb2[:, :], ident_c.ap(), "scn")
                    em.mark("bpjld")

                    def emit_tr(t):
                        em.waitck("tensor", ("x1c", t), "ve")
                        em.waitck("tensor", "bpjld", "scn")
                        for cc in range(NCC):
                            if cc >= 2:
                                em.waitck("tensor", ("x1tev", t, cc - 2), "sc")
                            else:
                                em.waitck("tensor", ("x1tev", t - 1, 6 + cc), "sc")
                            em.op("tensor", lambda e, cc=cc, t=t:
                                  e.transpose(ptx[cc % 2][:, :],
                                              x1c[t % 2][:, cc * 128:(cc + 1) * 128],
                                              identb2[:, :]),
                                  inc="mm")
                            em.mark(("x1tmm", t, cc))
                            em.waitck("scalar", ("x1tmm", t, cc), "mm")
                            em.op("scalar", lambda e, t=t, cc=cc:
                                  e.activation(x1t[:, cc, t * 128:(t + 1) * 128],
                                               ptx[cc % 2][:, :], AF.Copy),
                                  inc="sc")
                            em.mark(("x1tev", t, cc))

                    for t in range(16):
                        par = t % 2
                        pq = t % 3
                        bb, tt = t // 8, t % 8
                        em.waitck("tensor", ("onrm", bb), "ve")
                        em.waitck("tensor", "wproj", "swh2")
                        em.waitck("tensor", "wproj", "swh3")
                        em.waitck("tensor", ("pjfree", t - 3), "ve")
                        em.waitck("tensor", "bpjb", "swh")
                        for h in range(2):
                            for cc in range(NCC):
                                onsrc = on0m if bb == 0 else on1m
                                em.op("tensor", lambda e, h=h, cc=cc, tt=tt, pq=pq, onsrc=onsrc:
                                      e.matmul(pj[pq][:, h * 512:(h + 1) * 512],
                                               onsrc[:, cc, tt * 128:(tt + 1) * 128],
                                               wpm[:, cc, h * 512:(h + 1) * 512],
                                               start=(cc == 0), stop=False),
                                      inc=None)
                            em.op("tensor", lambda e, h=h, pq=pq:
                                  e.matmul(pj[pq][:, h * 512:(h + 1) * 512],
                                           onesr[:, :],
                                           bpjrow_b[:, h * 512:(h + 1) * 512],
                                           start=False, stop=True),
                                  inc=("mm" if h == 1 else None))
                        em.mark(("pjmm", t))

                        if t >= 2:
                            emit_tr(t - 2)

                        em.waitck("vector", ("pjmm", t), "mm")
                        em.waitck("vector", ("xst", t), "sa%d" % par)
                        em.op("vector", lambda e, t=t, par=par, pq=pq:
                              e.scalar_tensor_tensor(yacc[:, t, :], pj[pq][:, :],
                                                     1.0, xst[par][:, :],
                                                     ALU.mult, ALU.add,
                                                     accum_out=lnt[:, t, 0:1]),
                              inc="ve")
                        em.mark(("rtdone", t))
                        em.mark(("pjfree", t))
                        em.op("vector", lambda e, t=t:
                              e.scalar_tensor_tensor(sqb0[:, :], yacc[:, t, :],
                                                     1.0, yacc[:, t, :],
                                                     ALU.mult, ALU.mult,
                                                     accum_out=lnt[:, t, 1:2]),
                              inc="ve")
                        em.op("vector", lambda e, t=t:
                              e.tensor_scalar_mul(lnt[:, t, 2:3], lnt[:, t, 0:1],
                                                  -1.0 / C),
                              inc="ve")
                        em.op("vector", lambda e, t=t:
                              e.tensor_scalar_mul(lnt[:, t, 3:4], lnt[:, t, 1:2],
                                                  1.0 / C),
                              inc="ve")
                        em.op("vector", lambda e, t=t:
                              e.tensor_mul(lnt[:, t, 4:5], lnt[:, t, 2:3],
                                           lnt[:, t, 2:3]),
                              inc="ve")
                        em.op("vector", lambda e, t=t:
                              e.tensor_sub(lnt[:, t, 5:6], lnt[:, t, 3:4],
                                           lnt[:, t, 4:5]),
                              inc="ve")
                        em.mark(("var", t))
                        em.waitck("scalar", ("var", t), "ve")
                        em.op("scalar", lambda e, t=t:
                              e.activation(lnt[:, t, 6:7], lnt[:, t, 5:6], AF.Ln,
                                           bias=EPS),
                              inc="sc")
                        em.op("scalar", lambda e, t=t:
                              e.activation(lnt[:, t, 7:8], lnt[:, t, 6:7], AF.Exp,
                                           scale=-0.5),
                              inc="sc")
                        em.mark(("rstd", t))

                        if t + 2 < 16:
                            em.waitck("scalar", ("rtdone", t), "ve")
                            em.dma("scalar", xst[par][:, :],
                                   xown[(t + 2) * 128:(t + 3) * 128, :],
                                   "sa%d" % par)
                            em.mark(("xst", t + 2))
                        em.waitck("vector", ("rstd", t), "sc")
                        em.op("vector", lambda e, t=t:
                              e.tensor_mul(lnt[:, t, 6:7], lnt[:, t, 2:3],
                                           lnt[:, t, 7:8]),
                              inc="ve")
                        em.op("vector", lambda e, t=t, par=par:
                              e.tensor_scalar(yacc[:, t, :], yacc[:, t, :],
                                              lnt[:, t, 7:8], lnt[:, t, 6:7],
                                              ALU.mult, ALU.add),
                              inc="ve")
                        em.mark(("x1", t))
                        em.waitck("vector", ("x1tmm", t - 2, NCC - 1), "mm")
                        em.op("vector", lambda e, t=t, par=par:
                              e.tensor_copy(x1c[par][:, :], yacc[:, t, :]),
                              inc="ve")
                        em.mark(("x1c", t))
                        if t == 1:
                            Op_h = div_handles["Op"]
                            lsb_h = div_handles["lsb"]
                            em.op("vector", lambda e:
                                  e.reciprocal(lsb_h[:, :], lsb_h[:, :]),
                                  inc="ve")
                            for cc in range(NCC):
                                em.op("vector", lambda e, cc=cc:
                                      e.tensor_mul(on1m[:, cc, :],
                                                   Op_h[:, cc, :], lsb_h[:, :]),
                                      inc="ve")
                            em.mark(("onrm", 1))
                    emit_tr(14)
                    emit_tr(15)

                run_block(sched2)

            scopeC = contextlib.ExitStack()
            with scopeC:
                _a = scopeC.enter_context
                ph0 = _a(nc.psum_tensor("ph0", [128, 512], F32))
                ph1 = _a(nc.psum_tensor("ph1", [128, 512], F32))
                py0 = _a(nc.psum_tensor("py0", [128, C], F32))
                py1 = _a(nc.psum_tensor("py1", [128, C], F32))
                pmt = _a(nc.psum_tensor("pmt", [8, 128], BF16))
                ph = [ph0, ph1]
                py = [py0, py1]
                w1sb = [w1a, w1b]
                w2sb = [w2a, w2b]
                sqc = [sqc0, sqc1]

                def w1ap(e_, p):
                    return bass.AP(w1, e_ * C * C + p * 4 * 128 * C,
                                   [[C, 128], [128 * C, 4], [1, C]])

                def w2ap(e_, p):
                    return bass.AP(w2, e_ * C * C + p * 4 * 128 * C,
                                   [[C, 128], [128 * C, 4], [1, C]])

                def sched3():
                    em.waitck("sync", ("onrm", 1), "ve")
                    em.dma("sync", identb[:, :], ident_c.ap(), "scn")
                    em.mark("identb_ld")
                    em.dma("sync", b1_sb[:, :, :],
                           bass.AP(b1, 0, [[1, 128], [C, 8], [128, 8]]), "scn",
                           slow=True)
                    em.mark("ffn_small")
                    em.waitck("gpsimd", ("pjmm", 15), "mm")
                    em.dma("gpsimd", wg_sb[:, :, :],
                           bass.AP(wg, 0, [[8, 128], [128 * 8, NCC], [1, 8]]),
                           "swv")
                    em.dma("gpsimd", bgrow_b[:, :],
                           bass.AP(bg, 0, [[1, 1], [1, 8]]), "swh")
                    em.waitck("gpsimd", ("var", 15), "ve")
                    em.dma("gpsimd", b2r_sb[:, :],
                           bass.AP(b2, 0, [[C, 8], [1, C]]), "swh")
                    em.mark("gate_w")
                    for e_ in range(2):
                        em.dma("gpsimd", w1sb[e_][:, 0:4, :], w1ap(e_, 0),
                               "sw1%d" % e_)
                        em.dma("gpsimd", w1sb[e_][:, 4:8, :], w1ap(e_, 1),
                               "sw1%d" % e_)
                        em.mark(("w1", e_))
                        em.dma("gpsimd", w2sb[e_][:, 0:4, :], w2ap(e_, 0),
                               "sw2%d" % e_)
                        em.dma("gpsimd", w2sb[e_][:, 4:8, :], w2ap(e_, 1),
                               "sw2%d" % e_)
                        em.mark(("w2", e_))

                    em.waitck("tensor", "gate_w", "swv")
                    em.waitck("tensor", "gate_w", "swh")
                    em.waitck("tensor", "identb_ld", "scn")
                    for t in range(16):
                        par = t % 2
                        em.waitck("tensor", ("gev", t - 2), "sc")
                        for cc in range(NCC):
                            em.op("tensor", lambda e, t=t, cc=cc, par=par:
                                  e.matmul(py[par][:, 0:8],
                                           x1t[:, cc, t * 128:(t + 1) * 128],
                                           wg_sb[:, cc, :],
                                           start=(cc == 0), stop=False),
                                  inc=None)
                        em.op("tensor", lambda e, par=par:
                              e.matmul(py[par][:, 0:8], onesr[:, :], bgrow_b[:, :],
                                       start=False, stop=True),
                              inc="mm")
                        em.mark(("gmm", t))
                        em.waitck("scalar", ("gmm", t), "mm")
                        em.op("scalar", lambda e, t=t, par=par:
                              e.activation(lg[:, t, :], py[par][:, 0:8], AF.Copy),
                              inc="sc")
                        em.mark(("gev", t))
                        em.waitck("vector", ("gev", t), "sc")
                        em.op("vector", lambda e, t=t:
                              e.reduce_max(mxc[:, :], lg[:, t, :], axis=AX.X),
                              inc="ve")
                        em.op("vector", lambda e, t=t:
                              e.tensor_scalar(mstore[:, t, :], lg[:, t, :], mxc[:, :],
                                              None, ALU.is_equal),
                              inc="ve")
                        em.waitck("vector", ("mtr", t - 1), "mm")
                        em.op("vector", lambda e, t=t:
                              e.tensor_copy(mtb[:, :], mstore[:, t, :]),
                              inc="ve")
                        em.mark(("mtb", t))
                        em.waitck("tensor", ("mtb", t), "ve")
                        em.waitck("tensor", "identb_ld", "scn")
                        em.waitck("tensor", ("mtr_ev", t - 1), "sc")
                        em.op("tensor", lambda e: e.transpose(pmt[:, :], mtb[:, :],
                                                              identb[:, :]),
                              inc="mm")
                        em.mark(("mtr", t))
                        em.waitck("scalar", ("mtr", t), "mm")
                        em.op("scalar", lambda e, t=t:
                              e.activation(maskT[:, t, :], pmt[:, :], AF.Copy),
                              inc="sc")
                        em.mark(("mtr_ev", t))
                    em.mark("maskr")

                    def emit_b2pass():
                        for t in range(16):
                            par = t % 2
                            if t < 2:
                                em.waitck("tensor", ("gev", 14 + t), "sc")
                            else:
                                em.waitck("tensor", ("b2add", t - 2), "ve")
                            em.waitck("tensor", ("mtr_ev", t), "sc")
                            for h in range(2):
                                em.op("tensor", lambda e, t=t, h=h, par=par:
                                      e.matmul(py[par][:, h * 512:(h + 1) * 512],
                                               maskT[:, t, :],
                                               b2r_sb[:, h * 512:(h + 1) * 512],
                                               start=True, stop=True),
                                      inc=("mm" if h == 1 else None))
                            em.mark(("b2mm", t))
                            em.waitck("vector", ("b2mm", t), "mm")
                            em.op("vector", lambda e, t=t, par=par:
                                  e.tensor_add(yacc[:, t, :], yacc[:, t, :],
                                               py[par][:, :]),
                                  inc="ve")
                            em.mark(("b2add", t))

                    def ln2_stage1(t):
                        em.op("vector", lambda e, t=t:
                              e.scalar_tensor_tensor(sqc[t % 2][:, :], yacc[:, t, :],
                                                     1.0, yacc[:, t, :],
                                                     ALU.mult, ALU.mult,
                                                     accum_out=lnt2[:, t, 1:2]),
                              inc="ve")
                        em.op("vector", lambda e, t=t:
                              e.tensor_scalar_mul(lnt2[:, t, 2:3], lnt2[:, t, 0:1],
                                                  -1.0 / C),
                              inc="ve")
                        em.op("vector", lambda e, t=t:
                              e.tensor_scalar_mul(lnt2[:, t, 3:4], lnt2[:, t, 1:2],
                                                  1.0 / C),
                              inc="ve")
                        em.op("vector", lambda e, t=t:
                              e.tensor_mul(lnt2[:, t, 4:5], lnt2[:, t, 2:3],
                                           lnt2[:, t, 2:3]),
                              inc="ve")
                        em.op("vector", lambda e, t=t:
                              e.tensor_sub(lnt2[:, t, 5:6], lnt2[:, t, 3:4],
                                           lnt2[:, t, 4:5]),
                              inc="ve")
                        em.mark(("var2", t))
                        em.waitck("scalar", ("var2", t), "ve")
                        em.op("scalar", lambda e, t=t:
                              e.activation(lnt2[:, t, 6:7], lnt2[:, t, 5:6], AF.Ln,
                                           bias=EPS),
                              inc="sc")
                        em.op("scalar", lambda e, t=t:
                              e.activation(lnt2[:, t, 7:8], lnt2[:, t, 6:7], AF.Exp,
                                           scale=-0.5),
                              inc="sc")
                        em.mark(("rstd2", t))

                    def ln2_finish(t):
                        em.waitck("vector", ("rstd2", t), "sc")
                        em.op("vector", lambda e, t=t:
                              e.tensor_mul(lnt2[:, t, 6:7], lnt2[:, t, 2:3],
                                           lnt2[:, t, 7:8]),
                              inc="ve")
                        em.op("vector", lambda e, t=t:
                              e.tensor_scalar(yacc[:, t, :], yacc[:, t, :],
                                              lnt2[:, t, 7:8], lnt2[:, t, 6:7],
                                              ALU.mult, ALU.add),
                              inc="ve")
                        em.mark(("ofin", t))
                        em.waitck("scalar", ("ofin", t), "ve")
                        em.dma("scalar", out[t * 128:(t + 1) * 128, :],
                               yacc[:, t, :], "so%d" % (t % 8))
                        em.mark(("ostd", t))

                    em.waitck("scalar", "ffn_small", "scn")
                    for e_ in range(8):
                        em.waitck("tensor", ("w1", e_), "sw1%d" % (e_ % 2))
                        em.waitck("tensor", ("w1", e_), "sw1%db" % (e_ % 2))
                        for th in range(2):
                            for ic in range(NCC):
                                for tq in range(2):
                                    idx = ic * 2 + tq
                                    reg = idx % 2
                                    if idx < 2:
                                        if th == 1:
                                            em.waitck("tensor",
                                                      ("hev", e_ * 2, 14 + idx), "sc")
                                        else:
                                            em.waitck("tensor",
                                                      ("hev", (e_ - 1) * 2 + 1,
                                                       14 + idx), "sc")
                                    else:
                                        em.waitck("tensor",
                                                  ("hev", e_ * 2 + th, idx - 2), "sc")
                                    for cc in range(NCC):
                                        em.op("tensor",
                                              lambda e, ic=ic, tq=tq, cc=cc, th=th, reg=reg, e_=e_:
                                              e.matmul(ph[reg][:, :],
                                                       w1sb[e_ % 2][:, cc, ic * 128:(ic + 1) * 128],
                                                       x1t[:, cc, th * 1024 + tq * 512:
                                                           th * 1024 + (tq + 1) * 512],
                                                       start=(cc == 0), stop=(cc == 7)),
                                              inc=("mm" if cc == 7 else None))
                                    em.mark(("hmm", e_ * 2 + th, idx))
                                    em.waitck("scalar", ("hmm", e_ * 2 + th, idx), "mm")
                                    em.op("scalar", lambda e, ic=ic, tq=tq, reg=reg, e_=e_:
                                          e.activation(hsb[:, ic, tq * 512:(tq + 1) * 512],
                                                       ph[reg][:, :], AF.Gelu,
                                                       bias=b1_sb[:, e_, ic:ic + 1]),
                                          inc="sc")
                                    em.mark(("hev", e_ * 2 + th, idx))
                            if th == 1:
                                em.mark(("hdone", e_))
                                if e_ + 2 < 8:
                                    em.waitck("gpsimd", ("hdone", e_), "mm")
                                    em.dma("gpsimd", w1sb[e_ % 2][:, 0:4, :],
                                           w1ap(e_ + 2, 0), "sw1%d" % (e_ % 2))
                                    em.dma("gpsimd", w1sb[e_ % 2][:, 4:8, :],
                                           w1ap(e_ + 2, 1), "sw1%d" % (e_ % 2))
                                    em.mark(("w1", e_ + 2))

                            if e_ == 0 and th == 0:
                                emit_b2pass()

                            em.waitck("tensor", ("hev", e_ * 2 + th, 15), "sc")
                            em.waitck("tensor", ("w2", e_), "sw2%d" % (e_ % 2))
                            em.waitck("tensor", ("w2", e_), "sw2%db" % (e_ % 2))
                            for tt in range(8):
                                t = th * 8 + tt
                                par = tt % 2
                                if tt < 2:
                                    if e_ == 0 and th == 0:
                                        em.waitck("tensor", ("b2add", 14 + tt), "ve")
                                    else:
                                        em.waitck("tensor",
                                                  ("yadd", e_ * 2 + th - 1, 6 + tt),
                                                  "ve")
                                else:
                                    em.waitck("tensor",
                                              ("yadd", e_ * 2 + th, tt - 2), "ve")
                                for h in range(2):
                                    for ic in range(NCC):
                                        em.op("tensor",
                                              lambda e, tt=tt, h=h, ic=ic, par=par, e_=e_:
                                              e.matmul(py[par][:, h * 512:(h + 1) * 512],
                                                       hsb[:, ic, tt * 128:(tt + 1) * 128],
                                                       w2sb[e_ % 2][:, ic, h * 512:(h + 1) * 512],
                                                       start=(ic == 0), stop=(ic == 7)),
                                              inc=("mm" if (ic == 7 and h == 1) else None))
                                em.mark(("ymm", e_ * 2 + th, tt))
                                em.waitck("vector", ("ymm", e_ * 2 + th, tt), "mm")
                                if e_ == 7:
                                    em.op("vector", lambda e, t=t, par=par, e_=e_:
                                          e.scalar_tensor_tensor(
                                              yacc[:, t, :], py[par][:, :],
                                              mstore[:, t, e_:e_ + 1], yacc[:, t, :],
                                              ALU.mult, ALU.add,
                                              accum_out=lnt2[:, t, 0:1]),
                                          inc="ve")
                                else:
                                    em.op("vector", lambda e, t=t, par=par, e_=e_:
                                          e.scalar_tensor_tensor(
                                              yacc[:, t, :], py[par][:, :],
                                              mstore[:, t, e_:e_ + 1], yacc[:, t, :],
                                              ALU.mult, ALU.add),
                                          inc="ve")
                                em.mark(("yadd", e_ * 2 + th, tt))
                                if e_ == 7:
                                    ln2_stage1(t)
                                    if t >= 1:
                                        ln2_finish(t - 1)
                            if th == 1:
                                em.mark(("ydone", e_))
                                if e_ + 2 < 8:
                                    em.waitck("gpsimd", ("ydone", e_), "mm")
                                    em.dma("gpsimd", w2sb[e_ % 2][:, 0:4, :],
                                           w2ap(e_ + 2, 0), "sw2%d" % (e_ % 2))
                                    em.dma("gpsimd", w2sb[e_ % 2][:, 4:8, :],
                                           w2ap(e_ + 2, 1), "sw2%d" % (e_ % 2))
                                    em.mark(("w2", e_ + 2))

                    ln2_finish(15)
                    for _k in range(8):
                        em.wait("scalar", "so%d" % _k, 16 * 2)

                run_block(sched3)

    return nc


_NC_CACHE = {}


def kernel(L0_attn_x, L0_attn_w_attn, L0_attn_b_attn, L0_attn_w_proj,
           L0_attn_b_proj, L0_ffn_Wg, L0_ffn_bg, L0_ffn_W1, L0_ffn_b1,
           L0_ffn_W2, L0_ffn_b2):
    x = np.ascontiguousarray(np.asarray(L0_attn_x, dtype=np.float32))
    x2 = x.reshape(B * T, C)
    shared = {
        "wattn": np.ascontiguousarray(np.asarray(L0_attn_w_attn, np.float32)),
        "battn": np.ascontiguousarray(np.asarray(L0_attn_b_attn, np.float32)),
        "wproj": np.ascontiguousarray(np.asarray(L0_attn_w_proj, np.float32)),
        "bproj": np.ascontiguousarray(np.asarray(L0_attn_b_proj, np.float32)),
        "wg": np.ascontiguousarray(np.asarray(L0_ffn_Wg, np.float32)),
        "bg": np.ascontiguousarray(np.asarray(L0_ffn_bg, np.float32)),
        "w1": np.ascontiguousarray(np.asarray(L0_ffn_W1, np.float32)),
        "b1": np.ascontiguousarray(np.asarray(L0_ffn_b1, np.float32)),
        "w2": np.ascontiguousarray(np.asarray(L0_ffn_W2, np.float32)),
        "b2": np.ascontiguousarray(np.asarray(L0_ffn_b2, np.float32)),
    }

    xTp = [np.ascontiguousarray(x2[0:T].T),
           np.ascontiguousarray(x2[T:2 * T].T)]
    tri = np.triu(np.ones((128, 128), np.float32))
    in_maps = []
    for i in range(NCORE):
        blocks = _blocks_for_core(i)
        bA = i % 2
        view_batches = [bA, 1 - bA]
        rows = []
        for b in view_batches:
            for m in range(8):
                blk = blocks[m]
                rows.append(np.arange(blk * 128, (blk + 1) * 128) + b * T)
        rows = np.concatenate(rows)
        xo = np.ascontiguousarray(x2[rows])

        qm = np.zeros((64, 128, 128), np.float32)
        for k in range(64):
            a, r = k // 8, k % 8
            sig = i if a % 2 == 0 else 7 - i
            if sig > r:
                qm[k] = 1.0
            elif sig == r:
                qm[k] = tri
        qm = qm.reshape(64 * 128, 128).astype(BF)

        m = dict(shared)
        m["xT"] = xTp[bA]
        m["xown"] = xo
        m["xownT"] = np.ascontiguousarray(xo.T)
        m["qmask"] = np.ascontiguousarray(qm)
        in_maps.append(m)

    if "full" not in _NC_CACHE:
        _NC_CACHE["full"] = build()
    nc = _NC_CACHE["full"]

    res = run_bass_kernel_spmd(nc, in_maps, core_ids=list(range(NCORE)))

    outf = np.empty((B, T, C), np.float32)
    for i in range(NCORE):
        o = res.results[i]["out"]
        blocks = _blocks_for_core(i)
        bA = i % 2
        view_batches = [bA, 1 - bA]
        for vb, b in enumerate(view_batches):
            for m in range(8):
                blk = blocks[m]
                outf[b, blk * 128:(blk + 1) * 128, :] = \
                    o[(vb * 8 + m) * 128:(vb * 8 + m + 1) * 128, :]
    return outf
